# version 01
# baseline (speedup 1.0000x reference)
import numpy as np
import ml_dtypes
import contextlib
import concourse.bass as bass
import concourse.mybir as mybir
from concourse.bass_utils import run_bass_kernel_spmd

F32 = mybir.dt.float32
BF16 = mybir.dt.bfloat16
AF = mybir.ActivationFunctionType
ALU = mybir.AluOpType
AX = mybir.AxisListType
BF = ml_dtypes.bfloat16

L, C, NLT = 2048, 1024, 16
HA, DA, SN = 8, 128, 33
HF, DF, INNER, HID, HIDP, WIN = 8, 64, 512, 2730, 2816, 512
DEPTH = 2
NSL, SLW = 6, 455

def _band(s):
    st = s - 16
    if st > 0:
        lo, hi = st - 17, 16 * st + 18
    elif st < 0:
        lo, hi = 16 * st - 18, st + 18
    else:
        lo, hi = -18, 18
    return max(lo, -273), min(hi, 273)

BANDS = [_band(s) for s in range(SN)]
BW = [hi - lo + 1 for lo, hi in BANDS]
HOFF = np.cumsum([0] + BW)
HATW = int(HOFF[-1])
ADW = 547
ABW, ACH = 896, 7
SLOT = 1024
S_H, S_LT, S_P = SLOT, 8 * SLOT, 16 * 8 * SLOT
AD_OFF = 238
RD_C0 = AD_OFF - 111

_DEBUG = []


def _host_prep(inp):
    g = lambda k: np.ascontiguousarray(np.asarray(inp[k], dtype=np.float32))
    out = {}
    out["x"] = g("L0_attn_x")
    out["wavewT"] = np.ascontiguousarray(g("L0_attn_wave_w").T)
    out["wave_b"] = g("L0_attn_wave_b").reshape(1, 16)
    kw = g("L0_attn_kernel_w").reshape(HA, 64, C)[:, :SN, :].reshape(HA * SN, C)
    out["kernwT"] = np.ascontiguousarray(kw.T)
    out["kern_b"] = np.ascontiguousarray(g("L0_attn_kernel_b").reshape(HA, 64)[:, :SN].reshape(1, HA * SN))
    out["outwT"] = np.ascontiguousarray(g("L0_attn_out_w").T)
    out["wqkv"] = g("L0_ffn_wqkv")
    out["wo"] = g("L0_ffn_wo")
    out["w1"] = g("L0_ffn_w1")
    w2p = np.zeros((DEPTH, HIDP, C), np.float32)
    w2p[:, :HID, :] = g("L0_ffn_w2")
    out["w2"] = w2p
    out["qs"] = np.ascontiguousarray(g("L0_ffn_q_scale").reshape(DEPTH, DF, 1))
    out["ks"] = np.ascontiguousarray(g("L0_ffn_k_scale").reshape(DEPTH, DF, 1))
    fmcol = lambda v: np.ascontiguousarray(v.reshape(8, 128).T)
    out["ln_g"] = np.stack([fmcol(g("L0_ffn_ln_g")[l]) for l in range(DEPTH)])
    out["ln_b"] = np.stack([fmcol(g("L0_ffn_ln_b")[l]) for l in range(DEPTH)])
    out["b1"] = g("L0_ffn_b1").reshape(DEPTH, 1, 2 * HID)
    out["b2"] = np.stack([fmcol(g("L0_ffn_b2")[l]) for l in range(DEPTH)])
    return out


def _consts():
    cc = {}
    cc["ident_bf"] = np.eye(128, dtype=BF)
    cc["ident_f"] = np.eye(128, dtype=np.float32)
    cc["iota_ext"] = np.tile((np.arange(608, dtype=np.float32) - 304.0)[None, :], (128, 1))
    cc["lcol"] = (np.arange(NLT)[None, :] * 128 + np.arange(128)[:, None]).astype(np.float32)
    cc["srow"] = np.tile((np.arange(SN, dtype=np.float32) - 16.0)[None, :], (128, 1))
    r = np.arange(128)[:, None]
    cm = np.arange(128)[None, :]
    neg = np.float32(-3.0e38)
    cc["mask_lead"] = np.where(cm >= r, 0.0, neg).astype(np.float32)
    cc["mask_tail"] = np.where(cm <= r, 0.0, neg).astype(np.float32)
    half = np.arange(0, DF, 2, dtype=np.float32)
    inv_freq = 1.0 / 10000.0 ** (half / DF)
    t = np.arange(L, dtype=np.float32)
    freqs = np.concatenate([t[:, None] * inv_freq[None, :]] * 2, -1)
    base = (half + 0.4 * DF) / (1.4 * DF)
    power = (t - L // 2) / float(WIN // 2)
    scale = np.concatenate([base[None, :] ** power[:, None]] * 2, -1)
    cos, sin = np.cos(freqs), np.sin(freqs)
    sign = np.concatenate([-np.ones(DF // 2), np.ones(DF // 2)])[None, :]
    QK = 8.0
    cc["tabA"] = np.stack([(QK * cos * scale).T, (cos / scale).T]).astype(BF)
    cc["tabB"] = np.stack([(QK * sign * sin * scale).T, (sign * sin / scale).T]).astype(BF)
    cc["epscol"] = np.tile(np.array([[-1e-24, 1e-24]], np.float32), (128, 1))
    cc["ones_row"] = np.ones((1, 128), dtype=BF)
    cc["ones_col"] = np.ones((128, 1), dtype=BF)
    return cc


class Prog:
    def __init__(self, nc, es, block):
        self.nc, self.es, self.block = nc, es, block
        self.sems, self.cnt = {}, {}
        nc.alloc_sbuf_tensor("base_probe", [128, 16], F32)
        base = None
        for alloc in nc.m.functions[0].allocations:
            try:
                ml = alloc.memorylocations[0]
                if "base_probe" in ml.name:
                    base = ml.addr + 64
            except Exception:
                pass
        assert base is not None
        self.sbuf, self.sbuf_off, self.marks = {}, base, {}
        self.sbuf_peak = base

    def sem(self, name):
        if name not in self.sems:
            self.sems[name] = self.es.enter_context(self.nc.semaphore(name))
            self.cnt[name] = 0
        return self.sems[name]

    def inc(self, inst, name, n=1):
        try:
            eng = inst.ins.engine
            is_dma = "DMA" in type(inst.ins).__name__
        except Exception:
            eng, is_dma = None, True
        if not is_dma:
            if eng == mybir.EngineType.DVE:
                inst = self.nc.vector.drain()
            elif eng == mybir.EngineType.Activation:
                inst = self.nc.scalar.drain()
        inst.then_inc(self.sem(name), n)
        self.cnt[name] += n
        return self.cnt[name]

    def val(self, name):
        self.sem(name)
        return self.cnt[name]

    def wait(self, eng, name, value=None):
        self.sem(name)
        v = self.cnt[name] if value is None else value
        if v > 0:
            eng.wait_ge(self.sems[name], v)

    def alloc(self, name, shape, dtype, align=64):
        nbytes = int(np.prod(shape[1:])) * mybir.dt.size(dtype)
        off = (self.sbuf_off + align - 1) // align * align
        h = self.nc.alloc_sbuf_tensor_at(name, list(shape), dtype, offset=off)
        self.sbuf[name] = h
        self.sbuf_off = off + nbytes
        self.sbuf_peak = max(self.sbuf_peak, self.sbuf_off)
        return h

    def mark(self, name):
        self.marks[name] = self.sbuf_off

    def reset_to(self, name):
        self.sbuf_off = self.marks[name]

    def t(self, name):
        return self.sbuf[name]


def bcol(t, col, n):
    fs = int(np.prod(t.shape[1:]))
    return bass.AP(t, col, [[fs, 128], [0, n]])


def build_program():
    nc = bass.Bass("TRN2", target_bir_lowering=False)
    cc = _consts()
    x_in = nc.dram_tensor("x", [L, C], F32, kind="ExternalInput").ap()
    y_out = nc.dram_tensor("y", [L, C], F32, kind="ExternalOutput").ap()
    din = {}
    for nm, shp in [
        ("wavewT", [C, 16]), ("wave_b", [1, 16]), ("kernwT", [C, HA * SN]),
        ("kern_b", [1, HA * SN]), ("outwT", [C, C]),
        ("wqkv", [DEPTH, C, 3 * INNER]), ("wo", [DEPTH, INNER, C]),
        ("w1", [DEPTH, C, 2 * HID]), ("w2", [DEPTH, HIDP, C]),
        ("qs", [DEPTH, DF, 1]), ("ks", [DEPTH, DF, 1]),
        ("ln_g", [DEPTH, 128, 8]), ("ln_b", [DEPTH, 128, 8]),
        ("b1", [DEPTH, 1, 2 * HID]), ("b2", [DEPTH, 128, 8]),
    ]:
        din[nm] = nc.dram_tensor(nm, shp, F32, kind="ExternalInput").ap()
    x1_spill = nc.dram_tensor("x1spill", [L, C], F32, kind="Internal").ap()
    ad_hbm = nc.dram_tensor("adscr", [128 * S_P], BF16, kind="Internal").ap()
    h4_hbm = nc.dram_tensor("h4scr", [L, NSL * (SLW + 1)], BF16, kind="Internal").ap()
    con = {k: nc.inline_tensor(np.ascontiguousarray(v), name="c_" + k) for k, v in cc.items()}

    dbg_out = {}

    def dbg(name, shape, dtype=F32):
        if name in _DEBUG:
            dbg_out[name] = nc.dram_tensor("dbg_" + name, shape, dtype, kind="ExternalOutput").ap()
            dbg.cache[name] = dbg_out[name]
            return dbg_out[name]
        return None
    dbg.cache = {}

    es = contextlib.ExitStack()
    with es:
        block = es.enter_context(nc.Block())
        P = Prog(nc, es, block)
        P.psum = [es.enter_context(nc.psum_tensor(f"pb{i}", [128, 512], F32)) for i in range(5)]
        P.pS = es.enter_context(nc.psum_tensor("pS", [128, 1024], F32))
        P.psum.append(P.pS[:, 0:512])
        P.psum.append(P.pS[:, 512:1024])
        P.psum.append(es.enter_context(nc.psum_tensor("pb7", [128, 512], F32)))
        emit(P, x_in, y_out, din, con, x1_spill, ad_hbm, h4_hbm, dbg)
    return nc, sorted(dbg_out)


def emit(P, x_in, y_out, din, con, x1_spill, ad_hbm, h4_hbm, dbg):
    nc, block, psum = P.nc, P.block, P.psum

    def pbf(i):
        t = psum[i]
        ap = t[:] if hasattr(t, "num_partitions") or not isinstance(t, bass.AP) else t
        try:
            ap = t[:]
        except Exception:
            ap = t
        return ap.bitcast(BF16)

    for nm, shp, dt in [
        ("ident_bf", [128, 128], BF16), ("ident_f", [128, 128], F32),
        ("iota_ext", [128, 608], F32),
        ("lcol", [128, NLT], F32), ("srow", [128, SN], F32),
        ("mask_lead", [128, 128], F32), ("mask_tail", [128, 128], F32),
        ("epscol", [128, 2], F32),
        ("ones_row", [1, 128], BF16), ("ones_col", [128, 1], BF16),
        ("stat", [128, 96], F32), ("scr4k", [128, 1024], F32),
    ]:
        P.alloc(nm, shp, dt)
    P.mark("persist")

    x0 = P.alloc("x0", [128, NLT, C], F32)
    P.mark("after_x0")
    xh = P.alloc("xh", [128, NLT, C], BF16)
    wavewT = P.alloc("wavewT", [128, 8, 16], BF16)
    kernwT = P.alloc("kernwT", [128, 8, HA * SN], BF16)
    outwT = P.alloc("outwT", [128, 8, C], BF16)
    brow = P.alloc("brow", [1, 512], BF16)
    wtm = P.alloc("wtm", [128, NLT, HA * SN], F32)
    fa = P.alloc("fa", [128, NLT], F32)
    pa = P.alloc("pa", [128, NLT], F32)
    upp = P.alloc("upp", [128, NLT, SN], F32)
    wavescr = P.alloc("wavescr", [128, 32], F32)
    hats = P.alloc("hats", [128, HATW], BF16)
    ad2 = P.alloc("ad2", [128, 2, HA, ADW], BF16)
    aabs = P.alloc("aabs", [128, 2, ABW], BF16)
    akm = P.alloc("akm", [128, 2, ACH * 128], BF16)
    ohfm = P.alloc("ohfm", [128, 2, HA * 128], BF16)
    xfm_scr = P.alloc("xfm_scr", [128, 2, 8, 128], BF16)
    assert P.sbuf_peak < 229000, P.sbuf_peak

    @block.sync
    def _(sync):
        for nm in ["ident_bf", "ident_f", "iota_ext", "lcol", "srow", "mask_lead",
                   "mask_tail", "epscol", "ones_row", "ones_col"]:
            P.inc(sync.dma_start(out=P.t(nm)[:], in_=con[nm].ap()), "io", 16)
        P.inc(sync.dma_start(out=P.t("x0")[:], in_=x_in.rearrange("(lt p) c -> p lt c", p=128)), "io", 16)
    ld_x0 = P.val("io")

    @block.gpsimd
    def _(g):
        P.inc(g.dma_start(out=P.t("wavewT")[:], in_=din["wavewT"].rearrange("(ch p) n -> p ch n", p=128)), "wl", 16)
        P.inc(g.dma_start(out=P.t("kernwT")[:], in_=din["kernwT"].rearrange("(ch p) n -> p ch n", p=128)), "wl", 16)
        P.inc(g.dma_start(out=P.t("brow")[0:1, 0:16], in_=din["wave_b"]), "wl", 16)
        P.inc(g.dma_start(out=P.t("brow")[0:1, 16:16 + HA * SN], in_=din["kern_b"]), "wl", 16)
        P.inc(g.dma_start(out=P.t("outwT")[:], in_=din["outwT"].rearrange("(ch p) n -> p ch n", p=128)), "wl", 16)
    ld_attn_w = P.val("wl")

    emit_ln_tm(P, src=lambda lt: P.t("x0")[:, lt, :], dst=lambda lt: P.t("xh")[:, lt, :],
               wait_sem=("io", ld_x0))
    xh_done = P.val("v")

    emit_dbg_tap(P, dbg, "xh", P.t("xh")[:], [128, NLT, C], BF16, ("v", xh_done))
    emit_dbg_tap(P, dbg, "stat", P.t("stat")[:], [128, 96], F32, ("v", xh_done))

    prev_silu = None
    prev_fp = None
    for lt in range(NLT):
        b = lt % 2

        @block.tensor
        def _(t, lt=lt, b=b):
            if lt == 0:
                P.wait(t, "v", xh_done)
            for ch in range(8):
                bank = pbf(ch // 4)
                col = (ch % 4) * 128
                inst = t.transpose(bank[:, col:col + 128], P.t("xh")[:, lt, ch * 128:(ch + 1) * 128],
                                   P.t("ident_bf")[:])
            P.inc(inst, "pe")
        tr_done = P.val("pe")

        @block.vector
        def _(v, b=b, tr_done=tr_done):
            P.wait(v, "pe", tr_done)
            v.tensor_copy(P.t("xfm_scr")[:, b, 0:4, :].rearrange("p a c -> p (a c)"), pbf(0)[:, 0:512])
            P.inc(v.tensor_copy(P.t("xfm_scr")[:, b, 4:8, :].rearrange("p a c -> p (a c)"), pbf(1)[:, 0:512]), "v")
        evac_done = P.val("v")

        @block.tensor
        def _(t, lt=lt, b=b, evac_done=evac_done, prev_silu=prev_silu):
            P.wait(t, "v", evac_done)
            if lt == 0:
                P.wait(t, "wl", ld_attn_w)
            if prev_silu is not None:
                P.wait(t, "s", prev_silu)
            for ch in range(8):
                lhsT = P.t("xfm_scr")[:, b, ch, :]
                t.matmul(psum[2][:, 0:16], lhsT, P.t("wavewT")[:, ch, :], start=(ch == 0), stop=False)
            t.matmul(psum[2][:, 0:16], P.t("ones_row")[:], P.t("brow")[0:1, 0:16], start=False, stop=True)
            for ch in range(8):
                lhsT = P.t("xfm_scr")[:, b, ch, :]
                t.matmul(psum[3][:, 0:HA * SN], lhsT, P.t("kernwT")[:, ch, :], start=(ch == 0), stop=False)
            P.inc(t.matmul(psum[3][:, 0:HA * SN], P.t("ones_row")[:], P.t("brow")[0:1, 16:16 + HA * SN],
                           start=False, stop=True), "pe")
        mm_done = P.val("pe")

        @block.scalar
        def _(s, lt=lt, mm_done=mm_done, prev_fp=prev_fp):
            P.wait(s, "pe", mm_done)
            if prev_fp is not None:
                P.wait(s, "v", prev_fp)
            s.activation(out=P.t("wavescr")[:, 0:16], in_=psum[2][:, 0:16], func=AF.Silu)
            P.inc(s.activation(out=P.t("wtm")[:, lt, :], in_=psum[3][:, 0:HA * SN], func=AF.Silu), "s")
            s.activation(out=P.t("wavescr")[:, 0:8], in_=P.t("wavescr")[:, 0:8], func=AF.Sigmoid)
            P.inc(s.activation(out=P.t("wavescr")[:, 8:16], in_=P.t("wavescr")[:, 8:16], func=AF.Tanh), "s")
        prev_silu = P.val("s")

        @block.vector
        def _(v, lt=lt, prev_silu=prev_silu):
            P.wait(v, "s", prev_silu)
            v.tensor_reduce(out=P.t("fa")[:, lt:lt + 1], in_=P.t("wavescr")[:, 0:8], axis=AX.X, op=ALU.add)
            P.inc(v.tensor_reduce(out=P.t("pa")[:, lt:lt + 1], in_=P.t("wavescr")[:, 8:16],
                                  axis=AX.X, op=ALU.add), "v")
        prev_fp = P.val("v")

    @block.vector
    def _(v):
        P.wait(v, "v", P.val("v"))
        v.tensor_scalar(out=P.t("fa")[:, 0:NLT], in0=P.t("fa")[:, 0:NLT],
                        scalar1=15.0 / 8.0, scalar2=1.0, op0=ALU.mult, op1=ALU.add)
        v.tensor_scalar(out=P.t("pa")[:, 0:NLT], in0=P.t("pa")[:, 0:NLT],
                        scalar1=16.0 / 8.0, scalar2=0.0, op0=ALU.mult, op1=ALU.add)
        v.nop(cycle_cnt=64)
        for lt in range(NLT):
            v.tensor_scalar(out=P.t("upp")[:, lt, :], in0=P.t("srow")[:], scalar1=P.t("fa")[:, lt:lt + 1],
                            scalar2=P.t("pa")[:, lt:lt + 1], op0=ALU.mult, op1=ALU.add)
        for lt in range(NLT):
            v.tensor_scalar(out=P.t("upp")[:, lt, :], in0=P.t("upp")[:, lt, :],
                            scalar1=P.t("lcol")[:, lt:lt + 1], scalar2=2047.0, op0=ALU.add, op1=ALU.min)
        for lt in range(NLT):
            v.tensor_scalar(out=P.t("upp")[:, lt, :], in0=P.t("upp")[:, lt, :],
                            scalar1=0.0, scalar2=0.0, op0=ALU.max, op1=ALU.add)
        for lt in range(NLT):
            P.inc(v.tensor_scalar(out=P.t("upp")[:, lt, :], in0=P.t("upp")[:, lt, :],
                                  scalar1=P.t("lcol")[:, lt:lt + 1], scalar2=0.0,
                                  op0=ALU.subtract, op1=ALU.add), "v")
    prep_done = P.val("v")

    emit_dbg_tap(P, dbg, "wtm", P.t("wtm")[:], [128, NLT, HA * SN], F32, ("s", P.val("s")))
    emit_dbg_tap(P, dbg, "upp", P.t("upp")[:], [128, NLT, SN], F32, ("v", prep_done))

    hscr = P.alloc("hscr", [128, 2, 276], F32)

    zt = P.t("scr4k")[:].bitcast(BF16)

    @block.vector
    def _(v):
        P.inc(v.memset(P.t("scr4k")[:], 0.0), "v")
    z_done = P.val("v")

    @block.scalar
    def _(s):
        P.wait(s, "v", z_done)
        for k in range(64):
            P.inc(s.dma_start(out=ad_hbm.rearrange("(p f) -> p f", p=128)[:, k * 2048:(k + 1) * 2048],
                              in_=zt), "zz", 16)
    zero_hbm = P.val("zz")

    prev_fma = {}
    for lt in range(NLT):
        sub = lt % 2

        @block.scalar
        def _(s, lt=lt):
            if lt == 0:
                P.wait(s, "v", prep_done)
            else:
                P.wait(s, "v", prev_fma[lt - 1])
            for sv in range(SN):
                lo_m, hi_m = BANDS[sv]
                w = hi_m - lo_m + 1
                s.activation(out=P.t("hscr")[:, sv % 2, 0:w],
                             in_=P.t("iota_ext")[:, 304 + lo_m:304 + lo_m + w],
                             func=AF.Abs, scale=-1.0, bias=P.t("upp")[:, lt, sv:sv + 1])
                if sv > 0:
                    lo_p, hi_p = BANDS[sv - 1]
                    wp = hi_p - lo_p + 1
                    P.inc(s.activation(out=P.t("hats")[:, HOFF[sv - 1]:HOFF[sv - 1] + wp],
                                       in_=P.t("hscr")[:, (sv - 1) % 2, 0:wp],
                                       func=AF.Relu, scale=-1.0, bias=1.0), "hs")
            lo_p, hi_p = BANDS[SN - 1]
            wp = hi_p - lo_p + 1
            P.inc(s.activation(out=P.t("hats")[:, HOFF[SN - 1]:HOFF[SN - 1] + wp],
                               in_=P.t("hscr")[:, (SN - 1) % 2, 0:wp],
                               func=AF.Relu, scale=-1.0, bias=1.0), "hs")
        hs_done = P.val("hs")

        @block.gpsimd
        def _(g, lt=lt, sub=sub):
            if lt >= 2:
                P.wait(g, "adw", 16 * (lt - 1))
            if lt >= 1:
                P.wait(g, "v")
            P.inc(g.memset(P.t("ad2")[:, sub, :, :], 0.0), "adz")
        adz_done = P.val("adz")

        @block.vector
        def _(v, lt=lt, sub=sub, hs_done=hs_done, adz_done=adz_done):
            P.wait(v, "adz", adz_done)
            base = hs_done - SN
            for sv in range(SN):
                lo_m, hi_m = BANDS[sv]
                w = hi_m - lo_m + 1
                P.wait(v, "hs", base + sv + 1)
                for h in range(HA):
                    ad = P.t("ad2")[:, sub, h, lo_m + 273:lo_m + 273 + w]
                    inst = v.scalar_tensor_tensor(out=ad, in0=P.t("hats")[:, HOFF[sv]:HOFF[sv] + w],
                                                  scalar=P.t("wtm")[:, lt, h * SN + sv:h * SN + sv + 1],
                                                  in1=ad, op0=ALU.mult, op1=ALU.add)
            P.inc(inst, "v")
        fma_done = P.val("v")
        prev_fma[lt] = fma_done

        @block.gpsimd
        def _(g, lt=lt, sub=sub, fma_done=fma_done):
            P.wait(g, "v", fma_done)
            if lt == 0:
                P.wait(g, "zz", zero_hbm)
            dst = bass.AP(ad_hbm.tensor, lt * S_LT + AD_OFF, [[S_P, 128], [S_H, HA], [1, ADW]])
            P.inc(g.dma_start(out=dst, in_=P.t("ad2")[:, sub, :, :]), "adw", 16)
    adw_all = P.val("adw")

    for lt in range(NLT):
        b = lt % 2
        cvalid = [c for c in range(ACH) if 0 <= lt - 3 + c <= 15]

        for h in range(HA):
            hb = h % 2

            @block.sync
            def _(sync, lt=lt, h=h, hb=hb):
                if h == 0:
                    P.wait(sync, "adw", 16 * (lt + 1))
                P.wait(sync, "pe")
                src = bass.AP(ad_hbm.tensor, lt * S_LT + h * S_H + RD_C0, [[S_P - 1, 128], [1, ABW]])
                P.inc(sync.dma_start(out=P.t("aabs")[:, hb, :], in_=src), "shr", 16)
            shr_h = P.val("shr")

            @block.tensor
            def _(t, lt=lt, h=h, hb=hb, shr_h=shr_h, cvalid=cvalid):
                P.wait(t, "shr", shr_h)
                if lt > 0 or h > 1:
                    P.wait(t, "v")
                for c in cvalid:
                    inst = t.transpose(pbf(4)[:, c * 128:(c + 1) * 128],
                                       P.t("aabs")[:, hb, c * 128:(c + 1) * 128], P.t("ident_bf")[:])
                P.inc(inst, "pe")
            tp_done = P.val("pe")

            @block.vector
            def _(v, lt=lt, hb=hb, tp_done=tp_done, cvalid=cvalid):
                P.wait(v, "pe", tp_done)
                c0, c1 = cvalid[0], cvalid[-1]
                P.inc(v.tensor_copy(P.t("akm")[:, hb, c0 * 128:(c1 + 1) * 128],
                                    pbf(4)[:, c0 * 128:(c1 + 1) * 128]), "v")
            km_done = P.val("v")

            @block.tensor
            def _(t, lt=lt, h=h, hb=hb, km_done=km_done, cvalid=cvalid):
                P.wait(t, "v", km_done)
                bank = 0 if h < 4 else 1
                col = (h % 4) * 128
                for i, c in enumerate(cvalid):
                    inst = t.matmul(psum[bank][:, col:col + 128],
                                    P.t("xh")[:, lt - 3 + c, h * 128:(h + 1) * 128],
                                    P.t("akm")[:, hb, c * 128:(c + 1) * 128],
                                    start=(i == 0), stop=(i == len(cvalid) - 1))
                P.inc(inst, "pe")
        ein_done = P.val("pe")

        @block.vector
        def _(v, lt=lt, b=b, ein_done=ein_done):
            P.wait(v, "pe", ein_done)
            v.tensor_copy(P.t("ohfm")[:, b, 0:512], psum[0][:])
            P.inc(v.tensor_copy(P.t("ohfm")[:, b, 512:1024], psum[1][:]), "v")
        oh_done = P.val("v")

        @block.tensor
        def _(t, lt=lt, b=b, oh_done=oh_done):
            P.wait(t, "v", oh_done)
            P.wait(t, "s")
            for n2 in range(2):
                for hh in range(HA):
                    inst = t.matmul(psum[2 + n2][:, 0:512],
                                    P.t("ohfm")[:, b, hh * 128:(hh + 1) * 128],
                                    P.t("outwT")[:, hh, n2 * 512:(n2 + 1) * 512],
                                    start=(hh == 0), stop=(hh == HA - 1))
            P.inc(inst, "pe")
        ow_done = P.val("pe")

        @block.scalar
        def _(s, lt=lt, ow_done=ow_done):
            P.wait(s, "pe", ow_done)
            P.wait(s, "v")
            s.activation(out=P.t("scr4k")[:, 0:512], in_=psum[2][:], func=AF.Silu)
            P.inc(s.activation(out=P.t("scr4k")[:, 512:1024], in_=psum[3][:], func=AF.Silu), "s")
        sil_done = P.val("s")

        @block.vector
        def _(v, lt=lt, sil_done=sil_done):
            P.wait(v, "s", sil_done)
            P.inc(v.tensor_tensor(out=P.t("x0")[:, lt, :], in0=P.t("x0")[:, lt, :],
                                  in1=P.t("scr4k")[:], op=ALU.add), "v")
    x1_done = P.val("v")

    emit_dbg_tap(P, dbg, "x1", P.t("x0")[:], [128, NLT, C], F32, ("v", x1_done))

    P.reset_to("after_x0")
    zf = P.alloc("zf", [128, 8, 2052], F32)
    zcast = P.alloc("zcast", [128, 8, 128], BF16)
    mu_bf = P.alloc("mu_bf", [1, 2048], BF16)
    sig_bf = P.alloc("sig_bf", [1, 2048], BF16)
    invcol = P.alloc("invcol", [128, 16], F32)
    rz = P.alloc("rz", [128, 32], F32)
    qscol = P.alloc("qscol", [64, 4], F32)
    lgcol = P.alloc("lgcol", [128, 8], F32)
    lbcol = P.alloc("lbcol", [128, 8], BF16)
    b2col = P.alloc("b2col", [128, 8], F32)
    browf = P.alloc("browf", [1, 1024], F32)
    P.mark("ffn_dyn")
    vwin = P.alloc("vwin", [128, 5, INNER], BF16)
    kfm = P.alloc("kfm", [128, 4, 5, 128], BF16)
    qfm = P.alloc("qfm", [128, 4, 128], BF16)
    ofm = P.alloc("ofm", [128, 3, 4, 128], BF16)
    tabt = P.alloc("tabt", [128, 4, 64], BF16)
    tabscr = P.alloc("tabscr", [64, 4, 128], BF16)
    tabst = P.alloc("tabst", [64, 4, 128], BF16)
    qscr = P.alloc("qscr", [128, 2, INNER], F32)
    qrot = P.alloc("qrot", [128, 2, INNER], BF16)
    rotscr = P.alloc("rotscr", [128, 2, INNER], BF16)
    qkt = P.alloc("qkt", [128, 2, INNER], BF16)
    ptm = P.alloc("ptm", [128, 2, 640], BF16)
    pkm = P.alloc("pkm", [128, 2, 640], BF16)
    otm = P.alloc("otm", [128, INNER], BF16)
    attn_grp_end = P.sbuf_off
    P.reset_to("ffn_dyn")
    h4k = P.alloc("h4k", [128, NLT, SLW + 1], BF16)
    h4fmk = P.alloc("h4fmk", [128, 4, 1024], BF16)
    ggscr = P.alloc("ggscr", [128, SLW + 1], BF16)
    crow = P.alloc("crow", [1, 4, SLW + 1], BF16)
    nbrow = P.alloc("nbrow", [1, 4, SLW + 1], BF16)
    stB = P.alloc("stB", [128, 2, 2, 512], BF16)
    mlp_grp_end = P.sbuf_off
    P.sbuf_off = max(attn_grp_end, mlp_grp_end)
    x1t = P.alloc("x1t", [128, 2, C], F32)
    wq_off = P.marks["persist"]
    sv_off = P.sbuf_off
    P.sbuf_off = wq_off
    wqkv_sb = P.alloc("wqkv_sb", [128, 8, 3 * INNER], BF16)
    wo_sb = P.alloc("wo_sb", [128, 4, C], BF16)
    w1s = P.alloc("w1s", [128, 4, 8, SLW], BF16)
    assert P.sbuf_off <= P.marks["after_x0"], (P.sbuf_off, P.marks["after_x0"])
    w2_off = wq_off
    P.sbuf_off = w2_off
    w2s = P.alloc("w2s", [128, 2, 4, C], BF16)
    assert P.sbuf_off <= P.marks["after_x0"]
    P.sbuf_off = sv_off
    assert P.sbuf_peak < 229000, P.sbuf_peak

    emit_ln_tm(P, src=lambda lt: P.t("x0")[:, lt, :], dst=None, wait_sem=("v", x1_done))
    ln2_stats = P.val("s")

    @block.vector
    def _(v):
        P.inc(v.memset(zf[:, :, 0:1], 0.0), "v")

    @block.sync
    def _(sync):
        P.wait(sync, "v", x1_done)
        P.inc(sync.dma_start(out=x1_spill.rearrange("(lt p) c -> p lt c", p=128), in_=P.t("x0")[:]), "sp", 16)
    spill_done = P.val("sp")

    st = P.t("stat")
    for lt in range(NLT):
        @block.vector
        def _(v, lt=lt):
            if lt == 0:
                P.wait(v, "s", ln2_stats)
            P.wait(v, "pe")
            P.inc(v.scalar_tensor_tensor(out=P.t("scr4k")[:], in0=P.t("x0")[:, lt, :],
                                         scalar=st[:, lt:lt + 1], in1=bcol(st, 16 + lt, C),
                                         op0=ALU.add, op1=ALU.mult), "v")
        ap_done = P.val("v")

        @block.tensor
        def _(t, lt=lt, ap_done=ap_done):
            P.wait(t, "v", ap_done)
            for ch in range(8):
                bank = 0 if ch < 4 else 1
                col = (ch % 4) * 128
                inst = t.transpose(psum[bank][:, col:col + 128],
                                   P.t("scr4k")[:, ch * 128:(ch + 1) * 128],
                                   P.t("ident_f")[:])
            P.inc(inst, "pe")
        tp_done = P.val("pe")

        @block.vector
        def _(v, lt=lt, tp_done=tp_done):
            P.wait(v, "pe", tp_done)
            v.tensor_copy(zf[:, 0:4, 1 + lt * 128:1 + lt * 128 + 128], psum[0][:])
            P.inc(v.tensor_copy(zf[:, 4:8, 1 + lt * 128:1 + lt * 128 + 128], psum[1][:]), "v")
    z0_done = P.val("v")

    emit_dbg_tap(P, dbg, "z0", zf[:], [128, 8, 2052], F32, ("v", z0_done))

    for ly in range(DEPTH):
        emit_ffn_layer(P, ly, din, zf, psum, pbf, dbg, con, h4_hbm)

    zfin = P.val("v")

    @block.sync
    def _(sync):
        P.wait(sync, "sp", spill_done)
        P.wait(sync, "v", zfin)
        for b2 in range(2):
            P.inc(sync.dma_start(out=x1t[:, b2, :],
                                 in_=x1_spill.rearrange("(lt p) c -> p lt c", p=128)[:, b2, :]), "x1l", 16)

    for lt in range(NLT):
        b = lt % 2

        @block.tensor
        def _(t, lt=lt, zfin=zfin):
            if lt == 0:
                P.wait(t, "v", zfin)
                P.wait(t, "s")
            else:
                P.wait(t, "v")
            for ch in range(8):
                bank = 0 if ch < 4 else 1
                col = (ch % 4) * 128
                inst = t.transpose(psum[bank][:, col:col + 128],
                                   zf[:, ch, 1 + lt * 128:1 + lt * 128 + 128], P.t("ident_f")[:])
            P.inc(inst, "pe")
        g_tp = P.val("pe")

        @block.vector
        def _(v, lt=lt, b=b, g_tp=g_tp):
            P.wait(v, "pe", g_tp)
            P.wait(v, "x1l", 16 * (lt + 1))
            P.wait(v, "st")
            v.tensor_tensor(out=P.t("scr4k")[:, 0:512], in0=psum[0][:], in1=x1t[:, b, 0:512], op=ALU.add)
            P.inc(v.tensor_tensor(out=P.t("scr4k")[:, 512:1024], in0=psum[1][:], in1=x1t[:, b, 512:1024],
                                  op=ALU.add), "v")
        g_add = P.val("v")

        @block.sync
        def _(sync, lt=lt, b=b, g_add=g_add):
            P.wait(sync, "v", g_add)
            P.inc(sync.dma_start(out=y_out.rearrange("(lt p) c -> p lt c", p=128)[:, lt, :],
                                 in_=P.t("scr4k")[:, 0:1024]), "st", 16)
            if lt + 2 < NLT:
                P.inc(sync.dma_start(out=x1t[:, b, :],
                                     in_=x1_spill.rearrange("(lt p) c -> p lt c", p=128)[:, lt + 2, :]),
                      "x1l", 16)

    @block.sync
    def _(sync):
        sync.wait_ge(P.sem("st"), P.val("st"))



def emit_ffn_layer(P, ly, din, zf, psum, pbf, dbg, con, h4_hbm):
    block = P.block
    st = P.t("stat")
    scr = P.t("scr4k")

    @block.gpsimd
    def _(g):
        P.wait(g, "v")
        P.wait(g, "pe")
        P.inc(g.dma_start(out=P.t("wqkv_sb")[:], in_=din["wqkv"][ly].rearrange("(ch p) n -> p ch n", p=128)), "wl", 16)
        P.inc(g.dma_start(out=P.t("wo_sb")[:], in_=din["wo"][ly].rearrange("(ic p) n -> p ic n", p=128)), "wl", 16)
        P.inc(g.dma_start(out=P.t("qscol")[:, 0:1], in_=din["qs"][ly]), "wl", 16)
        P.inc(g.dma_start(out=P.t("qscol")[:, 2:3], in_=din["ks"][ly]), "wl", 16)
        P.inc(g.dma_start(out=P.t("lgcol")[:], in_=din["ln_g"][ly]), "wl", 16)
        P.inc(g.dma_start(out=P.t("lbcol")[:], in_=din["ln_b"][ly]), "wl", 16)
        P.inc(g.dma_start(out=P.t("b2col")[:], in_=din["b2"][ly]), "wl", 16)
    wl_base = P.val("wl")

    @block.vector
    def _(v):
        P.wait(v, "wl", wl_base)
        v.tensor_copy(P.t("qscol")[0:32, 1:2], P.t("qscol")[32:64, 0:1])
        v.tensor_copy(P.t("qscol")[0:32, 3:4], P.t("qscol")[32:64, 2:3])
        v.tensor_copy(P.t("qscol")[32:64, 1:2], P.t("qscol")[0:32, 0:1])
        P.inc(v.tensor_copy(P.t("qscol")[32:64, 3:4], P.t("qscol")[0:32, 2:3]), "v")
    qsw_done = P.val("v")

    for lt in range(NLT):
        l0 = lt * 128
        j0t = max(0, lt - 4)
        nkt = lt - j0t + 1

        @block.sync
        def _(sync, lt=lt, l0=l0):
            if lt == 0:
                P.wait(sync, "wl", wl_base)
            else:
                P.wait(sync, "v")
            P.inc(sync.dma_start(out=P.t("tabst")[:, 0:2, :],
                                 in_=con["tabA"].ap().rearrange("q d l -> d q l")[:, :, l0:l0 + 128]), "tb", 16)
            P.inc(sync.dma_start(out=P.t("tabst")[:, 2:4, :],
                                 in_=con["tabB"].ap().rearrange("q d l -> d q l")[:, :, l0:l0 + 128]), "tb", 16)
        tb_done = P.val("tb")

        @block.vector
        def _(v, lt=lt, tb_done=tb_done, qsw_done=qsw_done):
            P.wait(v, "tb", tb_done)
            if lt == 0:
                P.wait(v, "v", qsw_done)
            else:
                P.wait(v, "v")
            for i, (src_i, scol) in enumerate([(0, 0), (2, 1), (1, 2), (3, 3)]):
                inst = v.tensor_scalar(out=P.t("tabscr")[:, i, :], in0=P.t("tabst")[:, src_i, :],
                                       scalar1=P.t("qscol")[:, scol:scol + 1], scalar2=0.0,
                                       op0=ALU.mult, op1=ALU.add)
            P.inc(inst, "v")
        tsc_done = P.val("v")

        @block.vector
        def _(v, lt=lt, l0=l0):
            P.wait(v, "pe")
            v.tensor_copy(P.t("zcast")[:, 0:4, :], zf[:, 0:4, 1 + l0:1 + l0 + 128])
            P.inc(v.tensor_copy(P.t("zcast")[:, 4:8, :], zf[:, 4:8, l0:l0 + 128]), "v")
        zc_done = P.val("v")

        @block.tensor
        def _(t, lt=lt, zc_done=zc_done):
            P.wait(t, "v", zc_done)
            P.wait(t, "s")
            P.wait(t, "v")
            if lt == 0:
                P.wait(t, "wl", wl_base)
            for n in range(3):
                for ch in range(8):
                    inst = t.matmul(psum[n][:, 0:512], P.t("zcast")[:, ch, :],
                                    P.t("wqkv_sb")[:, ch, n * 512:(n + 1) * 512],
                                    start=(ch == 0), stop=(ch == 7))
            P.inc(inst, "pe")
        qkv_done = P.val("pe")

        @block.scalar
        def _(s, lt=lt, qkv_done=qkv_done):
            P.wait(s, "pe", qkv_done)
            P.wait(s, "v")
            s.activation(out=P.t("qscr")[:, 0, :], in_=psum[0][:], func=AF.Copy)
            P.inc(s.activation(out=P.t("qscr")[:, 1, :], in_=psum[1][:], func=AF.Copy), "s")
            s.activation(out=scr[:, 0:512], in_=psum[0][:], func=AF.Square)
            P.inc(s.activation(out=scr[:, 512:1024], in_=psum[1][:], func=AF.Square), "s")
        qk_done = P.val("s")

        @block.vector
        def _(v, lt=lt, qkv_done=qkv_done, qk_done=qk_done):
            P.wait(v, "pe", qkv_done)
            P.inc(v.tensor_copy(P.t("vwin")[:, lt % 5, :], psum[2][:]), "v")
            P.wait(v, "s", qk_done)
            v.tensor_reduce(out=P.t("rz")[:, 16:24], in_=scr[:, 0:512].rearrange("p (h d) -> p h d", h=8),
                            axis=AX.X, op=ALU.add)
            P.inc(v.tensor_reduce(out=P.t("rz")[:, 24:32], in_=scr[:, 512:1024].rearrange("p (h d) -> p h d", h=8),
                                  axis=AX.X, op=ALU.add), "v")
        ss_done = P.val("v")

        @block.scalar
        def _(s, lt=lt, ss_done=ss_done):
            P.wait(s, "v", ss_done)
            s.activation(out=st[:, 80:88], in_=P.t("rz")[:, 16:24], func=AF.Relu,
                         bias=P.t("epscol")[:, 0:1])
            s.activation(out=st[:, 88:96], in_=P.t("rz")[:, 24:32], func=AF.Relu,
                         bias=P.t("epscol")[:, 0:1])
            s.activation(out=st[:, 80:88], in_=st[:, 80:88], func=AF.Ln, bias=P.t("epscol")[:, 1:2])
            s.activation(out=st[:, 88:96], in_=st[:, 88:96], func=AF.Ln, bias=P.t("epscol")[:, 1:2])
            s.activation(out=st[:, 80:88], in_=st[:, 80:88], func=AF.Exp, scale=-0.5)
            P.inc(s.activation(out=st[:, 88:96], in_=st[:, 88:96], func=AF.Exp, scale=-0.5), "s")
        rn_done = P.val("s")

        @block.tensor
        def _(t, lt=lt, tsc_done=tsc_done):
            P.wait(t, "v", tsc_done)
            for i in range(4):
                inst = t.transpose(pbf(3)[:, i * 64:(i + 1) * 64], P.t("tabscr")[:, i, :],
                                   P.t("ident_bf")[0:64, 0:64])
            P.inc(inst, "pe")
        ttp_done = P.val("pe")

        @block.vector
        def _(v, lt=lt, ttp_done=ttp_done):
            P.wait(v, "pe", ttp_done)
            P.inc(v.tensor_copy(P.t("tabt")[:].rearrange("p a d -> p (a d)"), pbf(3)[:, 0:256]), "v")
        tabt_done = P.val("v")

        @block.vector
        def _(v, lt=lt, rn_done=rn_done, tabt_done=tabt_done):
            P.wait(v, "s", rn_done)
            P.wait(v, "v", tabt_done)
            rnq = bass.AP(st, 80, [[96, 128], [1, 8], [0, 64]])
            rnk = bass.AP(st, 88, [[96, 128], [1, 8], [0, 64]])
            v.scalar_tensor_tensor(out=P.t("qrot")[:, 0, :], in0=P.t("qscr")[:, 0, :], scalar=1.0,
                                   in1=rnq, op0=ALU.mult, op1=ALU.mult)
            v.scalar_tensor_tensor(out=P.t("qrot")[:, 1, :], in0=P.t("qscr")[:, 1, :], scalar=1.0,
                                   in1=rnk, op0=ALU.mult, op1=ALU.mult)
            tA_q = bass.AP(P.t("tabt"), 0, [[256, 128], [0, 8], [1, 64]])
            tB_q = bass.AP(P.t("tabt"), 64, [[256, 128], [0, 8], [1, 64]])
            tA_k = bass.AP(P.t("tabt"), 128, [[256, 128], [0, 8], [1, 64]])
            tB_k = bass.AP(P.t("tabt"), 192, [[256, 128], [0, 8], [1, 64]])
            v.scalar_tensor_tensor(out=P.t("qkt")[:, 0, :], in0=P.t("qrot")[:, 0, :], scalar=1.0,
                                   in1=tA_q, op0=ALU.mult, op1=ALU.mult)
            v.scalar_tensor_tensor(out=P.t("qkt")[:, 1, :], in0=P.t("qrot")[:, 1, :], scalar=1.0,
                                   in1=tA_k, op0=ALU.mult, op1=ALU.mult)
            def h32(t, off):
                return bass.AP(t, off, [[int(np.prod(t.shape[1:])), 128], [64, 8], [1, 32]])
            def tb32(off):
                return bass.AP(P.t("tabt"), off, [[256, 128], [0, 8], [1, 32]])
            qr, rs = P.t("qrot"), P.t("rotscr")
            v.scalar_tensor_tensor(out=h32(rs, 0), in0=h32(qr, 32), scalar=1.0,
                                   in1=tb32(64), op0=ALU.mult, op1=ALU.mult)
            v.scalar_tensor_tensor(out=h32(rs, 512), in0=h32(qr, 512 + 32), scalar=1.0,
                                   in1=tb32(192), op0=ALU.mult, op1=ALU.mult)
            v.scalar_tensor_tensor(out=h32(rs, 32), in0=h32(qr, 0), scalar=1.0,
                                   in1=tb32(64 + 32), op0=ALU.mult, op1=ALU.mult)
            v.scalar_tensor_tensor(out=h32(rs, 512 + 32), in0=h32(qr, 512), scalar=1.0,
                                   in1=tb32(192 + 32), op0=ALU.mult, op1=ALU.mult)
            v.tensor_tensor(out=P.t("qkt")[:, 0, :], in0=P.t("qkt")[:, 0, :],
                            in1=P.t("rotscr")[:, 0, :], op=ALU.add)
            P.inc(v.tensor_tensor(out=P.t("qkt")[:, 1, :], in0=P.t("qkt")[:, 1, :],
                                  in1=P.t("rotscr")[:, 1, :], op=ALU.add), "v")
        rot_done = P.val("v")

        d5 = dbg(f"snap{ly}", [128, 2, 1024], F32) if lt == 5 else None
        if d5 is not None:
            @block.vector
            def _(v, d5=d5):
                v.tensor_copy(P.t("x1t")[:, 0, :], P.t("qkt")[:].rearrange("p a d -> p (a d)"))
                P.inc(v.tensor_copy(P.t("x1t")[:, 1, :], P.t("qscr")[:].rearrange("p a d -> p (a d)")), "v")
            dbg.cache[f"snapreq{ly}"] = d5

        @block.tensor
        def _(t, lt=lt, rot_done=rot_done):
            P.wait(t, "v", rot_done)
            for hp in range(4):
                t.transpose(pbf(3)[:, hp * 128:(hp + 1) * 128],
                            P.t("qkt")[:, 0, hp * 128:(hp + 1) * 128], P.t("ident_bf")[:])
            for hp in range(4):
                inst = t.transpose(pbf(3)[:, 512 + hp * 128:512 + (hp + 1) * 128],
                                   P.t("qkt")[:, 1, hp * 128:(hp + 1) * 128], P.t("ident_bf")[:])
            P.inc(inst, "pe")
        qktp_done = P.val("pe")

        @block.vector
        def _(v, lt=lt, qktp_done=qktp_done):
            P.wait(v, "pe", qktp_done)
            v.tensor_copy(P.t("qfm")[:].rearrange("p a d -> p (a d)"), pbf(3)[:, 0:512])
            P.inc(v.tensor_copy(P.t("kfm")[:, :, lt % 5, :], pbf(3)[:, 512:1024]), "v")
        qkfm_done = P.val("v")

        for h in range(HF):
            hp, hh = h // 2, h % 2
            prow = slice(64 * hh, 64 * (hh + 1))

            @block.tensor
            def _(t, lt=lt, h=h, hp=hp, prow=prow, j0t=j0t, nkt=nkt, qkfm_done=qkfm_done):
                P.wait(t, "v", qkfm_done)
                P.wait(t, "s")
                if h == 0:
                    P.wait(t, "v")
                for i in range(nkt):
                    kt = j0t + i
                    bank, col = (5, i * 128) if i < 4 else (6, 0)
                    inst = t.matmul(psum[bank][:, col:col + 128],
                                    P.t("qfm")[prow, hp, :],
                                    P.t("kfm")[prow, hp, kt % 5, :],
                                    start=True, stop=True)
                P.inc(inst, "pe")
            sc_done = P.val("pe")

            @block.vector
            def _(v, lt=lt, h=h, nkt=nkt, sc_done=sc_done):
                P.wait(v, "pe", sc_done)
                tb_, tc_ = (5, (nkt - 1) * 128) if nkt - 1 < 4 else (6, 0)
                v.tensor_tensor(out=psum[tb_][:, tc_:tc_ + 128], in0=psum[tb_][:, tc_:tc_ + 128],
                                in1=P.t("mask_tail")[:], op=ALU.add)
                if lt >= 4:
                    v.tensor_tensor(out=psum[5][:, 0:128], in0=psum[5][:, 0:128],
                                    in1=P.t("mask_lead")[:], op=ALU.add)
                v.nop(cycle_cnt=64)
                P.inc(v.tensor_reduce(out=P.t("rz")[:, 10:11], in_=P.pS[:, 0:nkt * 128], axis=AX.X,
                                      op=ALU.max, negate=True), "v")
            msk_done = P.val("v")

            @block.scalar
            def _(s, lt=lt, h=h, nkt=nkt, msk_done=msk_done):
                P.wait(s, "v", msk_done)
                w5 = min(nkt, 4) * 128
                s.activation(out=P.t("ptm")[:, h % 2, 0:w5], in_=psum[5][:, 0:w5], func=AF.Exp,
                             bias=P.t("rz")[:, 10:11], accum_out=P.t("rz")[:, 11:12])
                if nkt == 5:
                    P.inc(s.activation(out=P.t("ptm")[:, h % 2, 512:640], in_=psum[6][:, 0:128],
                                       func=AF.Exp, bias=P.t("rz")[:, 10:11],
                                       accum_out=P.t("rz")[:, 12:13]), "s")
                else:
                    P.inc(s.activation(out=P.t("rz")[:, 12:13], in_=P.t("rz")[:, 11:12],
                                       func=AF.Copy, scale=0.0), "s")
            exp_done = P.val("s")

            d3 = dbg(f"hdbg{ly}", [128, NLT, HF, 8], F32) if (lt == 0 and h == 0) else dbg.cache.get(f"hdbg{ly}")
            if d3 is not None:
                @block.sync
                def _(sync, lt=lt, h=h, d3=d3, exp_done=exp_done):
                    P.wait(sync, "s", exp_done)
                    P.inc(sync.dma_start(out=d3[:, lt, h, :], in_=P.t("rz")[:, 8:16]), "st", 16)
            d4 = dbg(f"pdbg{ly}", [128, HF, 640], BF16) if (lt == 5 and h == 0) else (dbg.cache.get(f"pdbg{ly}") if lt == 5 else None)
            if d4 is not None:
                @block.sync
                def _(sync, h=h, d4=d4, exp_done=exp_done):
                    P.wait(sync, "s", exp_done)
                    P.inc(sync.dma_start(out=d4[:, h, :], in_=P.t("ptm")[:, h % 2, :]), "st", 16)

            @block.vector
            def _(v, h=h, exp_done=exp_done):
                P.wait(v, "s", exp_done)
                P.inc(v.tensor_tensor(out=P.t("rz")[:, h:h + 1], in0=P.t("rz")[:, 11:12],
                                      in1=P.t("rz")[:, 12:13], op=ALU.add), "v")
            zs_done = P.val("v")

            @block.tensor
            def _(t, lt=lt, h=h, nkt=nkt, zs_done=zs_done):
                P.wait(t, "v", zs_done)
                P.wait(t, "v")
                for i in range(nkt):
                    col = i * 128
                    src = P.t("ptm")[:, h % 2, col:col + 128] if i < 4 else P.t("ptm")[:, h % 2, 512:640]
                    inst = t.transpose(pbf(4)[:, col:col + 128], src, P.t("ident_bf")[:])
                P.inc(inst, "pe")
            ptp_done = P.val("pe")

            @block.vector
            def _(v, h=h, nkt=nkt, ptp_done=ptp_done):
                P.wait(v, "pe", ptp_done)
                P.inc(v.tensor_copy(P.t("pkm")[:, h % 2, 0:nkt * 128], pbf(4)[:, 0:nkt * 128]), "v")
            pkm_done = P.val("v")

            @block.tensor
            def _(t, lt=lt, h=h, j0t=j0t, nkt=nkt, pkm_done=pkm_done):
                P.wait(t, "v", pkm_done)
                for i in range(nkt):
                    kt = j0t + i
                    inst = t.matmul(psum[7][:, h * 64:(h + 1) * 64],
                                    P.t("pkm")[:, h % 2, i * 128:(i + 1) * 128],
                                    P.t("vwin")[:, kt % 5, h * 64:(h + 1) * 64],
                                    start=(i == 0), stop=(i == nkt - 1))
                P.inc(inst, "pe")
        av_done = P.val("pe")

        @block.vector
        def _(v, lt=lt, av_done=av_done):
            P.wait(v, "pe", av_done)
            v.nop(cycle_cnt=32)
            v.reciprocal(out=P.t("rz")[:, 0:8], in_=P.t("rz")[:, 0:8])
            v.nop(cycle_cnt=32)
            rzb = bass.AP(P.t("rz"), 0, [[32, 128], [1, 8], [0, 64]])
            P.inc(v.scalar_tensor_tensor(out=P.t("otm")[:], in0=psum[7][:], scalar=1.0,
                                         in1=rzb, op0=ALU.mult, op1=ALU.mult), "v")
        onorm_done = P.val("v")

        d = dbg(f"otm{ly}", [128, NLT, INNER], BF16) if lt == 0 else dbg.cache.get(f"otm{ly}")
        if d is not None:
            @block.sync
            def _(sync, lt=lt, d=d, onorm_done=onorm_done):
                P.wait(sync, "v", onorm_done)
                P.inc(sync.dma_start(out=d[:, lt, :], in_=P.t("otm")[:]), "st", 16)
        d2 = dbg(f"rza{ly}", [128, NLT, 32], F32) if lt == 0 else dbg.cache.get(f"rza{ly}")
        if d2 is not None:
            @block.sync
            def _(sync, lt=lt, d2=d2, onorm_done=onorm_done):
                P.wait(sync, "v", onorm_done)
                P.inc(sync.dma_start(out=d2[:, lt, :], in_=P.t("rz")[:]), "st", 16)

        @block.tensor
        def _(t, lt=lt, onorm_done=onorm_done):
            P.wait(t, "v", onorm_done)
            P.wait(t, "st")
            for ic in range(4):
                inst = t.transpose(pbf(4)[:, ic * 128:(ic + 1) * 128],
                                   P.t("otm")[:, ic * 128:(ic + 1) * 128], P.t("ident_bf")[:])
            P.inc(inst, "pe")
        otp_done = P.val("pe")

        @block.vector
        def _(v, lt=lt, otp_done=otp_done):
            P.wait(v, "pe", otp_done)
            P.inc(v.tensor_copy(P.t("ofm")[:, lt % 3, :, :].rearrange("p a d -> p (a d)"),
                                pbf(4)[:, 0:512]), "v")

        if lt >= 2:
            emit_wo_update(P, lt - 2, zf, psum)
    emit_wo_update(P, NLT - 2, zf, psum)
    emit_wo_update(P, NLT - 1, zf, psum)
    attn_done = P.val("v")

    emit_dbg_tap(P, dbg, f"zl{ly}a", zf[:], [128, 8, 2052], F32, ("v", attn_done))
    dsnap = dbg.cache.get(f"snapreq{ly}")
    if dsnap is not None:
        @block.sync
        def _(sync, dsnap=dsnap):
            P.wait(sync, "v", attn_done)
            P.inc(sync.dma_start(out=dsnap, in_=P.t("x1t")[:]), "st", 16)
    dsnap2 = dbg(f"mlpsnap{ly}", [128, 2, 1024], F32) if getattr(P, "snap_mlp", None) == ly else None
    P.snap_pending = dsnap2

    emit_ffn_stats(P, zf, psum, pbf)

    for pair in range(NSL):
        emit_w1_pair(P, ly, pair, din, zf, psum, pbf, h4_hbm)
    for pair in range(NSL):
        emit_w2_pair(P, ly, pair, din, zf, psum, pbf, h4_hbm)

    if getattr(P, "snap_pending", None) is not None:
        dsp = P.snap_pending

        @block.sync
        def _(sync, dsp=dsp):
            P.wait(sync, "v", getattr(P, "snapdone", P.val("v")))
            P.inc(sync.dma_start(out=dsp, in_=P.t("x1t")[:]), "st", 16)
        P.snap_pending = None

    emit_dbg_tap(P, dbg, f"zl{ly}", zf[:], [128, 8, 2052], F32, ("v", P.val("v")))


def emit_wo_update(P, lt, zf, psum):
    block = P.block
    l0 = lt * 128

    @block.tensor
    def _(t, lt=lt):
        P.wait(t, "v")
        P.wait(t, "s")
        for half in range(2):
            bank = 5 if half == 0 else 6
            for cq in range(4):
                ch = half * 4 + cq
                for ic in range(4):
                    inst = t.matmul(psum[bank][:, cq * 128:(cq + 1) * 128],
                                    P.t("wo_sb")[:, ic, ch * 128:(ch + 1) * 128],
                                    P.t("ofm")[:, lt % 3, ic, :],
                                    start=(ic == 0), stop=(ic == 3))
        P.inc(inst, "pe")
    wo_done = P.val("pe")

    @block.vector
    def _(v, lt=lt, l0=l0, wo_done=wo_done):
        P.wait(v, "pe", wo_done)
        v.tensor_tensor(out=zf[:, 0:4, 1 + l0:1 + l0 + 128], in0=zf[:, 0:4, 1 + l0:1 + l0 + 128],
                        in1=psum[5][:].rearrange("p (a d) -> p a d", a=4), op=ALU.add)
        P.inc(v.tensor_tensor(out=zf[:, 4:8, 1 + l0:1 + l0 + 128], in0=zf[:, 4:8, 1 + l0:1 + l0 + 128],
                              in1=psum[6][:].rearrange("p (a d) -> p a d", a=4), op=ALU.add), "v")


def emit_ffn_stats(P, zf, psum, pbf):
    block = P.block
    scr = P.t("scr4k")
    base_v = P.val("v")

    for tg in range(4):
        c0 = 1 + tg * 512
        for ch in range(8):
            cc = c0 if ch < 4 else c0 - 1
            sl = ch % 2

            @block.vector
            def _(v, tg=tg, ch=ch, cc=cc, sl=sl, base_v=base_v):
                if tg == 0 and ch == 0:
                    P.wait(v, "v", base_v)
                P.wait(v, "pe")
                P.inc(v.tensor_copy(P.t("stB")[:, 0, sl, :], zf[:, ch, cc:cc + 512]), "v")
            zcB = P.val("v")

            @block.scalar
            def _(s, tg=tg, ch=ch, cc=cc, sl=sl):
                P.wait(s, "pe")
                P.inc(s.activation(out=P.t("stB")[:, 1, sl, :], in_=zf[:, ch, cc:cc + 512],
                                   func=AF.Square), "s")
            sqB = P.val("s")

            @block.tensor
            def _(t, tg=tg, ch=ch, sl=sl, zcB=zcB, sqB=sqB):
                P.wait(t, "v", zcB)
                P.wait(t, "s", sqB)
                if ch == 0:
                    P.wait(t, "v")
                t.matmul(psum[0][0:1, :], P.t("ones_col")[:], P.t("stB")[:, 0, sl, :],
                         start=(ch == 0), stop=(ch == 7))
                inst = t.matmul(psum[1][0:1, :], P.t("ones_col")[:], P.t("stB")[:, 1, sl, :],
                                start=(ch == 0), stop=(ch == 7))
                P.inc(inst, "pe")
        mm_done = P.val("pe")

        @block.vector
        def _(v, tg=tg, mm_done=mm_done):
            P.wait(v, "pe", mm_done)
            v.tensor_scalar(out=scr[0:1, 0:512], in0=psum[0][0:1, :], scalar1=1.0 / C, scalar2=0.0,
                            op0=ALU.mult, op1=ALU.add)
            v.tensor_scalar(out=scr[0:1, 512:1024], in0=psum[1][0:1, :], scalar1=1.0 / C, scalar2=1e-5,
                            op0=ALU.mult, op1=ALU.add)
            v.tensor_copy(P.t("mu_bf")[0:1, tg * 512:(tg + 1) * 512], scr[0:1, 0:512])
            v.nop(cycle_cnt=32)
            v.tensor_tensor(out=scr[0:1, 0:512], in0=scr[0:1, 0:512], in1=scr[0:1, 0:512],
                            op=ALU.mult)
            v.nop(cycle_cnt=32)
            P.inc(v.tensor_tensor(out=scr[0:1, 512:1024], in0=scr[0:1, 512:1024],
                                  in1=scr[0:1, 0:512], op=ALU.subtract), "v")
        var_done = P.val("v")

        @block.scalar
        def _(s, tg=tg, var_done=var_done):
            P.wait(s, "v", var_done)
            s.activation(out=scr[0:1, 0:512], in_=scr[0:1, 512:1024], func=AF.Ln)
            s.activation(out=P.t("sig_bf")[0:1, tg * 512:(tg + 1) * 512], in_=scr[0:1, 0:512],
                         func=AF.Exp, scale=0.5)
            P.inc(s.activation(out=scr[0:1, 512:1024], in_=scr[0:1, 0:512],
                               func=AF.Exp, scale=-0.5), "s")
        inv_done = P.val("s")

        @block.tensor
        def _(t, tg=tg, inv_done=inv_done):
            P.wait(t, "s", inv_done)
            if tg > 0:
                P.wait(t, "v")
            for i in range(4):
                inst = t.transpose(psum[3][0:128, i:i + 1],
                                   scr[0:1, 512 + i * 128:512 + (i + 1) * 128],
                                   P.t("ident_f")[0:1, 0:1])
            P.inc(inst, "pe")
        ic_done = P.val("pe")

        @block.vector
        def _(v, tg=tg, ic_done=ic_done):
            P.wait(v, "pe", ic_done)
            P.inc(v.tensor_copy(P.t("invcol")[:, tg * 4:(tg + 1) * 4], psum[3][0:128, 0:4]), "v")


def emit_w1_pair(P, ly, pair, din, zf, psum, pbf, h4_hbm):
    block = P.block
    scr = P.t("scr4k")
    par = pair % 2
    sa, sg = 2 * par, 2 * par + 1
    a0, g0 = pair * SLW, HID + pair * SLW

    @block.gpsimd
    def _(g):
        P.wait(g, "pe")
        P.inc(g.dma_start(out=P.t("w1s")[:, sa, :, :],
                          in_=din["w1"][ly][:, a0:a0 + SLW].rearrange("(ch p) n -> p ch n", p=128)), "wl", 16)
        P.inc(g.dma_start(out=P.t("w1s")[:, sg, :, :],
                          in_=din["w1"][ly][:, g0:g0 + SLW].rearrange("(ch p) n -> p ch n", p=128)), "wl", 16)
    wsl_done = P.val("wl")

    @block.sync
    def _(sync):
        P.wait(sync, "v")
        P.inc(sync.dma_start(out=scr[0:1, 0:SLW], in_=din["b1"][ly][:, a0:a0 + SLW]), "b1", 16)
        P.inc(sync.dma_start(out=scr[0:1, 512:512 + SLW], in_=din["b1"][ly][:, g0:g0 + SLW]), "b1", 16)
    b1_done = P.val("b1")

    @block.tensor
    def _(t, wsl_done=wsl_done):
        P.wait(t, "wl", wsl_done)
        P.wait(t, "v")
        for ch in range(8):
            t.matmul(psum[0][0:1, 0:SLW], P.t("lbcol")[:, ch:ch + 1], P.t("w1s")[:, sa, ch, :],
                     start=(ch == 0), stop=(ch == 7))
            inst = t.matmul(psum[1][0:1, 0:SLW], P.t("lbcol")[:, ch:ch + 1], P.t("w1s")[:, sg, ch, :],
                            start=(ch == 0), stop=(ch == 7))
        P.inc(inst, "pe")
    cmm_done = P.val("pe")

    @block.vector
    def _(v, b1_done=b1_done, cmm_done=cmm_done):
        P.wait(v, "b1", b1_done)
        P.wait(v, "pe", cmm_done)
        v.tensor_tensor(out=P.t("crow")[0:1, 2 * par, :SLW], in0=psum[0][0:1, 0:SLW],
                        in1=scr[0:1, 0:SLW], op=ALU.add)
        P.inc(v.tensor_tensor(out=P.t("crow")[0:1, 2 * par + 1, :SLW], in0=psum[1][0:1, 0:SLW],
                              in1=scr[0:1, 512:512 + SLW], op=ALU.add), "v")
        for ch in range(8):
            v.tensor_scalar(out=P.t("w1s")[:, sa, ch, :], in0=P.t("w1s")[:, sa, ch, :],
                            scalar1=P.t("lgcol")[:, ch:ch + 1], scalar2=0.0, op0=ALU.mult, op1=ALU.add)
            inst = v.tensor_scalar(out=P.t("w1s")[:, sg, ch, :], in0=P.t("w1s")[:, sg, ch, :],
                                   scalar1=P.t("lgcol")[:, ch:ch + 1], scalar2=0.0, op0=ALU.mult, op1=ALU.add)
        P.inc(inst, "v")
    scale_done = P.val("v")

    @block.tensor
    def _(t, scale_done=scale_done):
        P.wait(t, "v", scale_done)
        for ch in range(8):
            t.matmul(psum[0][0:1, 0:SLW], P.t("ones_col")[:], P.t("w1s")[:, sa, ch, :],
                     start=(ch == 0), stop=(ch == 7))
            inst = t.matmul(psum[1][0:1, 0:SLW], P.t("ones_col")[:], P.t("w1s")[:, sg, ch, :],
                            start=(ch == 0), stop=(ch == 7))
        P.inc(inst, "pe")
    bmm_done = P.val("pe")

    @block.vector
    def _(v, bmm_done=bmm_done):
        P.wait(v, "pe", bmm_done)
        v.tensor_scalar(out=P.t("nbrow")[0:1, 2 * par, :SLW], in0=psum[0][0:1, 0:SLW],
                        scalar1=-1.0, scalar2=0.0, op0=ALU.mult, op1=ALU.add)
        P.inc(v.tensor_scalar(out=P.t("nbrow")[0:1, 2 * par + 1, :SLW], in0=psum[1][0:1, 0:SLW],
                              scalar1=-1.0, scalar2=0.0, op0=ALU.mult, op1=ALU.add), "v")
    rows_done = P.val("v")

    for lt in range(NLT):
        l0 = lt * 128

        @block.vector
        def _(v, lt=lt, l0=l0):
            P.wait(v, "s")
            v.tensor_copy(P.t("zcast")[:, 0:4, :], zf[:, 0:4, 1 + l0:1 + l0 + 128])
            P.inc(v.tensor_copy(P.t("zcast")[:, 4:8, :], zf[:, 4:8, l0:l0 + 128]), "v")
            if lt == 5 and getattr(P, "snap_mlp", None) == ly and pair == getattr(P, "snap_pair", 0):
                P.inc(v.tensor_copy(P.t("stB")[:].rearrange("p a b d -> p (a b d)")[:, 0:1024],
                                    P.t("zcast")[:].rearrange("p a d -> p (a d)")), "v")
        zc_done = P.val("v")

        @block.tensor
        def _(t, lt=lt, l0=l0, zc_done=zc_done, rows_done=rows_done):
            P.wait(t, "v", zc_done)
            if lt == 0:
                P.wait(t, "v", rows_done)
            P.wait(t, "s")
            for sl, bank in ((sa, 5), (sg, 6)):
                for ch in range(8):
                    t.matmul(psum[bank][:, 0:SLW], P.t("zcast")[:, ch, :], P.t("w1s")[:, sl, ch, :],
                             start=(ch == 0), stop=False)
                t.matmul(psum[bank][:, 0:SLW], P.t("mu_bf")[0:1, l0:l0 + 128],
                         P.t("nbrow")[0:1, sl, :SLW], start=False, stop=False)
                inst = t.matmul(psum[bank][:, 0:SLW], P.t("sig_bf")[0:1, l0:l0 + 128],
                                P.t("crow")[0:1, sl, :SLW], start=False, stop=True)
            P.inc(inst, "pe")
        amm_done = P.val("pe")

        @block.scalar
        def _(s, lt=lt, amm_done=amm_done):
            P.wait(s, "pe", amm_done)
            P.wait(s, "v")
            if lt == 5 and getattr(P, "snap_mlp", None) == ly and pair == getattr(P, "snap_pair", 0):
                s.activation(out=P.t("x1t")[:, 0, 0:SLW], in_=psum[5][:, 0:SLW], func=AF.Copy)
                s.activation(out=P.t("x1t")[:, 1, 0:SLW], in_=psum[6][:, 0:SLW], func=AF.Copy)
            P.inc(s.activation(out=P.t("ggscr")[:, 0:SLW], in_=psum[6][:, 0:SLW], func=AF.Gelu,
                               scale=P.t("invcol")[:, lt:lt + 1]), "s")
        gg_done = P.val("s")

        @block.vector
        def _(v, lt=lt, gg_done=gg_done):
            P.wait(v, "s", gg_done)
            P.inc(v.scalar_tensor_tensor(out=P.t("h4k")[:, lt, 0:SLW], in0=psum[5][:, 0:SLW],
                                         scalar=P.t("invcol")[:, lt:lt + 1],
                                         in1=P.t("ggscr")[:, 0:SLW], op0=ALU.mult, op1=ALU.mult), "v")
    h4_done = P.val("v")

    if getattr(P, "snap_mlp", None) == ly and pair == getattr(P, "snap_pair", 0):
        dh4 = P.nc.dram_tensor(f"dbg_h4dump{pair}", [128, NLT, SLW + 1], BF16, kind="ExternalOutput").ap()

        dps = P.nc.dram_tensor(f"dbg_psdump{pair}", [128, 2, 1024], F32, kind="ExternalOutput").ap()
        dw1 = P.nc.dram_tensor(f"dbg_w1dump{pair}", [128, 2, 8, SLW], BF16, kind="ExternalOutput").ap()

        @block.sync
        def _(sync, dh4=dh4, dps=dps, dw1=dw1):
            P.wait(sync, "v", h4_done)
            P.inc(sync.dma_start(out=dh4, in_=P.t("h4k")[:]), "st", 16)
            P.inc(sync.dma_start(out=dps, in_=P.t("x1t")[:]), "st", 16)
            P.inc(sync.dma_start(out=dw1, in_=P.t("w1s")[:, 2 * (pair % 2):2 * (pair % 2) + 2, :, :]), "st", 16)
            dzc = P.nc.dram_tensor(f"dbg_zcdump{pair}", [128, 1024], BF16, kind="ExternalOutput").ap()
            P.inc(sync.dma_start(out=dzc, in_=P.t("stB")[:].rearrange("p a b d -> p (a b d)")[:, 0:1024]), "st", 16)
        st_h4 = P.val("st")

        @block.vector
        def _(v, st_h4=st_h4):
            P.wait(v, "st", st_h4)
            P.inc(v.tensor_copy(P.t("x1t")[:, 1, 456:472], P.t("invcol")[:]), "v")
        P.snapdone = P.val("v")

    @block.sync
    def _(sync):
        P.wait(sync, "v", h4_done)
        P.inc(sync.dma_start(out=h4_hbm.rearrange("(lt p) n -> p lt n", p=128)[:, :, pair * (SLW + 1):(pair + 1) * (SLW + 1)],
                             in_=P.t("h4k")[:]), "h4s", 16)


def emit_w2_pair(P, ly, pair, din, zf, psum, pbf, h4_hbm):
    block = P.block
    par = pair % 2

    @block.gpsimd
    def _(g):
        P.wait(g, "pe")
        r0 = pair * SLW
        for sub in range(4):
            rs = r0 + sub * 128
            rn = min(128, r0 + SLW - rs)
            P.inc(g.dma_start(out=P.t("w2s")[0:rn, par, sub, :], in_=din["w2"][ly][rs:rs + rn, :]), "wl", 16)

    @block.sync
    def _(sync):
        P.wait(sync, "wl")
        P.wait(sync, "h4s", 16 * (NSL * ly + pair + 1))
        P.wait(sync, "pe")
        P.inc(sync.dma_start(out=P.t("h4k")[:],
                             in_=h4_hbm.rearrange("(lt p) n -> p lt n", p=128)[:, :, pair * (SLW + 1):(pair + 1) * (SLW + 1)]), "h4l", 16)
    h4l_done = P.val("h4l")

    for half in range(2):
        for ltt in range(8):
            lt = half * 8 + ltt

            @block.tensor
            def _(t, lt=lt, ltt=ltt, h4l_done=h4l_done):
                if ltt == 0:
                    P.wait(t, "h4l", h4l_done)
                P.wait(t, "v")
                for sub in range(4):
                    w = min(128, SLW - sub * 128)
                    inst = t.transpose(pbf(4)[0:w, sub * 128:(sub + 1) * 128],
                                       P.t("h4k")[:, lt, sub * 128:sub * 128 + w],
                                       P.t("ident_bf")[:])
                P.inc(inst, "pe")
            htp_done = P.val("pe")

            @block.vector
            def _(v, lt=lt, ltt=ltt, htp_done=htp_done):
                P.wait(v, "pe", htp_done)
                for sub in range(4):
                    w = min(128, SLW - sub * 128)
                    inst = v.tensor_copy(P.t("h4fmk")[0:w, sub, ltt * 128:(ltt + 1) * 128],
                                         pbf(4)[0:w, sub * 128:(sub + 1) * 128])
                P.inc(inst, "v")
        hfm_done = P.val("v")

        for tg in range(2):
            for ch in range(8):
                bank = 5 if (tg * 8 + ch) % 2 == 0 else 6

                @block.tensor
                def _(t, tg=tg, ch=ch, bank=bank, hfm_done=hfm_done):
                    P.wait(t, "v", hfm_done)
                    P.wait(t, "v")
                    P.wait(t, "s")
                    for sub in range(4):
                        w = min(128, SLW - sub * 128)
                        inst = t.matmul(psum[bank][:, 0:512],
                                        P.t("w2s")[0:w, par, sub, ch * 128:(ch + 1) * 128],
                                        P.t("h4fmk")[0:w, sub, tg * 512:(tg + 1) * 512],
                                        start=(sub == 0), stop=(sub == 3))
                    P.inc(inst, "pe")
                wmm = P.val("pe")
                emit_w2_add(P, half, tg, ch, zf, psum, bank, wmm,
                            last_pair=(pair == NSL - 1))


def emit_w2_add(P, half, tg, ch, zf, psum, bank, wmm, last_pair):
    block = P.block
    c0 = 1 + half * 1024 + tg * 512

    @block.vector
    def _(v, half=half, tg=tg, ch=ch, bank=bank, wmm=wmm):
        P.wait(v, "pe", wmm)
        if last_pair:
            P.inc(v.scalar_tensor_tensor(out=zf[:, ch, c0:c0 + 512], in0=psum[bank][:, 0:512],
                                         scalar=P.t("b2col")[:, ch:ch + 1], in1=zf[:, ch, c0:c0 + 512],
                                         op0=ALU.add, op1=ALU.add), "v")
        else:
            P.inc(v.tensor_tensor(out=zf[:, ch, c0:c0 + 512], in0=zf[:, ch, c0:c0 + 512],
                                  in1=psum[bank][:, 0:512], op=ALU.add), "v")


def emit_ln_tm(P, src, dst, wait_sem, out_bf16=True):
    st = P.t("stat")
    H = [(0, 8), (8, 16)]

    @P.block.vector
    def _(v):
        P.wait(v, *wait_sem)
        for lt in range(NLT):
            inst = v.tensor_reduce(out=st[:, lt:lt + 1], in_=src(lt), axis=AX.X, op=ALU.add, negate=True)
        P.inc(inst, "v")

    @P.block.scalar
    def _(s):
        P.wait(s, *wait_sem)
        for lt in range(NLT):
            P.inc(s.activation(out=P.t("scr4k")[:], in_=src(lt), func=AF.Square,
                               accum_out=st[:, 32 + lt:33 + lt]), "s")
    sq_done = P.val("s")

    @P.block.vector
    def _(v):
        P.wait(v, "s", sq_done)
        for a, b in H:
            v.tensor_scalar(out=st[:, a:b], in0=st[:, a:b], scalar1=1.0 / C, scalar2=0.0,
                            op0=ALU.mult, op1=ALU.add)
        v.nop(cycle_cnt=64)
        for a, b in H:
            v.tensor_tensor(out=st[:, 64 + a:64 + b], in0=st[:, a:b], in1=st[:, a:b], op=ALU.mult)
        for a, b in H:
            v.tensor_scalar(out=st[:, 48 + a:48 + b], in0=st[:, 32 + a:32 + b],
                            scalar1=1.0 / C, scalar2=1e-5, op0=ALU.mult, op1=ALU.add)
        for a, b in H:
            P.inc(v.tensor_tensor(out=st[:, 48 + a:48 + b], in0=st[:, 48 + a:48 + b],
                                  in1=st[:, 64 + a:64 + b], op=ALU.subtract), "v")
    var_done = P.val("v")

    @P.block.scalar
    def _(s):
        P.wait(s, "v", var_done)
        for a, b in H:
            s.activation(out=st[:, 64 + a:64 + b], in_=st[:, 48 + a:48 + b], func=AF.Ln)
        for a, b in H:
            P.inc(s.activation(out=st[:, 16 + a:16 + b], in_=st[:, 64 + a:64 + b],
                               func=AF.Exp, scale=-0.5), "s")
    sig_done = P.val("s")

    @P.block.vector
    def _(v):
        P.wait(v, "s", sig_done)
        if dst is not None:
            for lt in range(NLT):
                P.inc(v.scalar_tensor_tensor(out=dst(lt), in0=src(lt), scalar=st[:, lt:lt + 1],
                                             in1=bcol(P.t("stat"), 16 + lt, C), op0=ALU.add, op1=ALU.mult), "v")


def emit_dbg_tap(P, dbg, name, src_ap, shape, dtype, wait):
    d = dbg(name, shape, dtype)
    if d is None:
        return

    @P.block.sync
    def _(sync):
        P.wait(sync, *wait)
        P.inc(sync.dma_start(out=d, in_=src_ap), "st", 16)


def kernel(**inputs):
    hp = _host_prep(inputs)
    nc, dbg_names = build_program()
    x = hp.pop("x")
    in_maps = []
    for c in range(8):
        m = {"x": np.ascontiguousarray(x[c])}
        m.update(hp)
        in_maps.append(m)
    res = run_bass_kernel_spmd(nc, in_maps, core_ids=list(range(8)))
    out = np.stack([np.asarray(res.results[i]["y"]) for i in range(8)])
    if _DEBUG:
        kernel._dbg = [{k[4:]: np.asarray(v) for k, v in r.items() if k.startswith("dbg_")}
                       for r in res.results]
    return out


# version 02 (final)
# speedup vs baseline: 1.0421x; 1.0421x over previous
import numpy as np
import ml_dtypes
import contextlib
import concourse.bass as bass
import concourse.mybir as mybir
from concourse.bass_utils import run_bass_kernel_spmd

F32 = mybir.dt.float32
BF16 = mybir.dt.bfloat16
AF = mybir.ActivationFunctionType
ALU = mybir.AluOpType
AX = mybir.AxisListType
BF = ml_dtypes.bfloat16

L, C, NLT = 2048, 1024, 16
HA, DA, SN = 8, 128, 33
HF, DF, INNER, HID, HIDP, WIN = 8, 64, 512, 2730, 2816, 512
DEPTH = 2
NSL, SLW = 6, 455

def _band(s):
    st = s - 16
    if st > 0:
        lo, hi = st - 17, 16 * st + 18
    elif st < 0:
        lo, hi = 16 * st - 18, st + 18
    else:
        lo, hi = -18, 18
    return max(lo, -273), min(hi, 273)

BANDS = [_band(s) for s in range(SN)]
BW = [hi - lo + 1 for lo, hi in BANDS]
HOFF = np.cumsum([0] + BW)
HATW = int(HOFF[-1])
ADW = 547
ABW, ACH = 896, 7
SLOT = 1024
S_H, S_LT, S_P = SLOT, 8 * SLOT, 16 * 8 * SLOT
AD_OFF = 238
RD_C0 = AD_OFF - 111

_DEBUG = []


def _host_prep(inp):
    g = lambda k: np.ascontiguousarray(np.asarray(inp[k], dtype=np.float32))
    out = {}
    out["x"] = g("L0_attn_x")
    out["wavewT"] = np.ascontiguousarray(g("L0_attn_wave_w").T)
    out["wave_b"] = g("L0_attn_wave_b").reshape(1, 16)
    kw = g("L0_attn_kernel_w").reshape(HA, 64, C)[:, :SN, :].reshape(HA * SN, C)
    out["kernwT"] = np.ascontiguousarray(kw.T)
    out["kern_b"] = np.ascontiguousarray(g("L0_attn_kernel_b").reshape(HA, 64)[:, :SN].reshape(1, HA * SN))
    out["outwT"] = np.ascontiguousarray(g("L0_attn_out_w").T)
    out["wqkv"] = g("L0_ffn_wqkv")
    out["wo"] = g("L0_ffn_wo")
    out["w1"] = g("L0_ffn_w1")
    w2p = np.zeros((DEPTH, HIDP, C), np.float32)
    w2p[:, :HID, :] = g("L0_ffn_w2")
    out["w2"] = w2p
    out["qs"] = np.ascontiguousarray(g("L0_ffn_q_scale").reshape(DEPTH, DF, 1))
    out["ks"] = np.ascontiguousarray(g("L0_ffn_k_scale").reshape(DEPTH, DF, 1))
    fmcol = lambda v: np.ascontiguousarray(v.reshape(8, 128).T)
    out["ln_g"] = np.stack([fmcol(g("L0_ffn_ln_g")[l]) for l in range(DEPTH)])
    out["ln_b"] = np.stack([fmcol(g("L0_ffn_ln_b")[l]) for l in range(DEPTH)])
    out["b1"] = g("L0_ffn_b1").reshape(DEPTH, 1, 2 * HID)
    out["b2"] = np.stack([fmcol(g("L0_ffn_b2")[l]) for l in range(DEPTH)])
    return out


def _consts():
    cc = {}
    cc["ident_bf"] = np.eye(128, dtype=BF)
    cc["ident_f"] = np.eye(128, dtype=np.float32)
    cc["iota_ext"] = np.tile((np.arange(608, dtype=np.float32) - 304.0)[None, :], (128, 1))
    cc["lcol"] = (np.arange(NLT)[None, :] * 128 + np.arange(128)[:, None]).astype(np.float32)
    cc["srow"] = np.tile((np.arange(SN, dtype=np.float32) - 16.0)[None, :], (128, 1))
    r = np.arange(128)[:, None]
    cm = np.arange(128)[None, :]
    neg = np.float32(-3.0e38)
    cc["mask_lead"] = np.where(cm >= r, 0.0, neg).astype(np.float32)
    cc["mask_tail"] = np.where(cm <= r, 0.0, neg).astype(np.float32)
    half = np.arange(0, DF, 2, dtype=np.float32)
    inv_freq = 1.0 / 10000.0 ** (half / DF)
    t = np.arange(L, dtype=np.float32)
    freqs = np.concatenate([t[:, None] * inv_freq[None, :]] * 2, -1)
    base = (half + 0.4 * DF) / (1.4 * DF)
    power = (t - L // 2) / float(WIN // 2)
    scale = np.concatenate([base[None, :] ** power[:, None]] * 2, -1)
    cos, sin = np.cos(freqs), np.sin(freqs)
    sign = np.concatenate([-np.ones(DF // 2), np.ones(DF // 2)])[None, :]
    QK = 8.0
    cc["tabA"] = np.stack([(QK * cos * scale).T, (cos / scale).T]).astype(BF)
    cc["tabB"] = np.stack([(QK * sign * sin * scale).T, (sign * sin / scale).T]).astype(BF)
    cc["epscol"] = np.tile(np.array([[-1e-24, 1e-24]], np.float32), (128, 1))
    cc["ones_row"] = np.ones((1, 128), dtype=BF)
    cc["ones_col"] = np.ones((128, 1), dtype=BF)
    return cc


class Prog:
    def __init__(self, nc, es, block):
        self.nc, self.es, self.block = nc, es, block
        self.sems, self.cnt = {}, {}
        nc.alloc_sbuf_tensor("base_probe", [128, 16], F32)
        base = None
        for alloc in nc.m.functions[0].allocations:
            try:
                ml = alloc.memorylocations[0]
                if "base_probe" in ml.name:
                    base = ml.addr + 64
            except Exception:
                pass
        assert base is not None
        self.sbuf, self.sbuf_off, self.marks = {}, base, {}
        self.sbuf_peak = base

    def sem(self, name):
        if name not in self.sems:
            self.sems[name] = self.es.enter_context(self.nc.semaphore(name))
            self.cnt[name] = 0
        return self.sems[name]

    def inc(self, inst, name, n=1):
        try:
            eng = inst.ins.engine
            is_dma = "DMA" in type(inst.ins).__name__
        except Exception:
            eng, is_dma = None, True
        if not is_dma:
            if eng == mybir.EngineType.DVE:
                inst = self.nc.vector.drain()
            elif eng == mybir.EngineType.Activation:
                inst = self.nc.scalar.drain()
        inst.then_inc(self.sem(name), n)
        self.cnt[name] += n
        return self.cnt[name]

    def val(self, name):
        self.sem(name)
        return self.cnt[name]

    def wait(self, eng, name, value=None):
        self.sem(name)
        v = self.cnt[name] if value is None else value
        if v > 0:
            eng.wait_ge(self.sems[name], v)

    def alloc(self, name, shape, dtype, align=64):
        nbytes = int(np.prod(shape[1:])) * mybir.dt.size(dtype)
        off = (self.sbuf_off + align - 1) // align * align
        h = self.nc.alloc_sbuf_tensor_at(name, list(shape), dtype, offset=off)
        self.sbuf[name] = h
        self.sbuf_off = off + nbytes
        self.sbuf_peak = max(self.sbuf_peak, self.sbuf_off)
        return h

    def mark(self, name):
        self.marks[name] = self.sbuf_off

    def reset_to(self, name):
        self.sbuf_off = self.marks[name]

    def t(self, name):
        return self.sbuf[name]


def bcol(t, col, n):
    fs = int(np.prod(t.shape[1:]))
    return bass.AP(t, col, [[fs, 128], [0, n]])


def build_program():
    nc = bass.Bass("TRN2", target_bir_lowering=False)
    cc = _consts()
    x_in = nc.dram_tensor("x", [L, C], F32, kind="ExternalInput").ap()
    y_out = nc.dram_tensor("y", [L, C], F32, kind="ExternalOutput").ap()
    din = {}
    for nm, shp in [
        ("wavewT", [C, 16]), ("wave_b", [1, 16]), ("kernwT", [C, HA * SN]),
        ("kern_b", [1, HA * SN]), ("outwT", [C, C]),
        ("wqkv", [DEPTH, C, 3 * INNER]), ("wo", [DEPTH, INNER, C]),
        ("w1", [DEPTH, C, 2 * HID]), ("w2", [DEPTH, HIDP, C]),
        ("qs", [DEPTH, DF, 1]), ("ks", [DEPTH, DF, 1]),
        ("ln_g", [DEPTH, 128, 8]), ("ln_b", [DEPTH, 128, 8]),
        ("b1", [DEPTH, 1, 2 * HID]), ("b2", [DEPTH, 128, 8]),
    ]:
        din[nm] = nc.dram_tensor(nm, shp, F32, kind="ExternalInput").ap()
    x1_spill = nc.dram_tensor("x1spill", [L, C], F32, kind="Internal").ap()
    ad_hbm = nc.dram_tensor("adscr", [128 * S_P], BF16, kind="Internal").ap()
    h4_hbm = nc.dram_tensor("h4scr", [L, NSL * (SLW + 1)], BF16, kind="Internal").ap()
    con = {k: nc.inline_tensor(np.ascontiguousarray(v), name="c_" + k) for k, v in cc.items()}

    dbg_out = {}

    def dbg(name, shape, dtype=F32):
        if name in _DEBUG:
            dbg_out[name] = nc.dram_tensor("dbg_" + name, shape, dtype, kind="ExternalOutput").ap()
            dbg.cache[name] = dbg_out[name]
            return dbg_out[name]
        return None
    dbg.cache = {}

    es = contextlib.ExitStack()
    with es:
        block = es.enter_context(nc.Block())
        P = Prog(nc, es, block)
        P.psum = [es.enter_context(nc.psum_tensor(f"pb{i}", [128, 512], F32)) for i in range(5)]
        P.pS = es.enter_context(nc.psum_tensor("pS", [128, 1024], F32))
        P.psum.append(P.pS[:, 0:512])
        P.psum.append(P.pS[:, 512:1024])
        P.psum.append(es.enter_context(nc.psum_tensor("pb7", [128, 512], F32)))
        emit(P, x_in, y_out, din, con, x1_spill, ad_hbm, h4_hbm, dbg)
    return nc, sorted(dbg_out)


def emit(P, x_in, y_out, din, con, x1_spill, ad_hbm, h4_hbm, dbg):
    nc, block, psum = P.nc, P.block, P.psum

    def pbf(i):
        t = psum[i]
        ap = t[:] if hasattr(t, "num_partitions") or not isinstance(t, bass.AP) else t
        try:
            ap = t[:]
        except Exception:
            ap = t
        return ap.bitcast(BF16)

    for nm, shp, dt in [
        ("ident_bf", [128, 128], BF16), ("ident_f", [128, 128], F32),
        ("iota_ext", [128, 608], F32),
        ("lcol", [128, NLT], F32), ("srow", [128, SN], F32),
        ("mask_lead", [128, 128], F32), ("mask_tail", [128, 128], F32),
        ("epscol", [128, 2], F32),
        ("ones_row", [1, 128], BF16), ("ones_col", [128, 1], BF16),
        ("stat", [128, 96], F32), ("scr4k", [128, 1024], F32),
    ]:
        P.alloc(nm, shp, dt)
    P.mark("persist")

    x0 = P.alloc("x0", [128, NLT, C], F32)
    P.mark("after_x0")
    xh = P.alloc("xh", [128, NLT, C], BF16)
    wavewT = P.alloc("wavewT", [128, 8, 16], BF16)
    kernwT = P.alloc("kernwT", [128, 8, HA * SN], BF16)
    outwT = P.alloc("outwT", [128, 8, C], BF16)
    brow = P.alloc("brow", [1, 512], BF16)
    wtm = P.alloc("wtm", [128, NLT, HA * SN], F32)
    fa = P.alloc("fa", [128, NLT], F32)
    pa = P.alloc("pa", [128, NLT], F32)
    upp = P.alloc("upp", [128, NLT, SN], F32)
    wavescr = P.alloc("wavescr", [128, 32], F32)
    hats = P.alloc("hats", [128, HATW], BF16)
    ad2 = P.alloc("ad2", [128, 2, HA, ADW], BF16)
    aabs = P.alloc("aabs", [128, 2, ABW], BF16)
    akm = P.alloc("akm", [128, 2, ACH * 128], BF16)
    ohfm = P.alloc("ohfm", [128, 2, HA * 128], BF16)
    xfm_scr = P.alloc("xfm_scr", [128, 2, 8, 128], BF16)
    assert P.sbuf_peak < 229000, P.sbuf_peak

    @block.sync
    def _(sync):
        for nm in ["ident_bf", "ident_f", "iota_ext", "lcol", "srow", "mask_lead",
                   "mask_tail", "epscol", "ones_row", "ones_col"]:
            P.inc(sync.dma_start(out=P.t(nm)[:], in_=con[nm].ap()), "io", 16)
        P.inc(sync.dma_start(out=P.t("x0")[:], in_=x_in.rearrange("(lt p) c -> p lt c", p=128)), "io", 16)
    ld_x0 = P.val("io")

    @block.gpsimd
    def _(g):
        P.inc(g.dma_start(out=P.t("wavewT")[:], in_=din["wavewT"].rearrange("(ch p) n -> p ch n", p=128)), "wl", 16)
        P.inc(g.dma_start(out=P.t("kernwT")[:], in_=din["kernwT"].rearrange("(ch p) n -> p ch n", p=128)), "wl", 16)
        P.inc(g.dma_start(out=P.t("brow")[0:1, 0:16], in_=din["wave_b"]), "wl", 16)
        P.inc(g.dma_start(out=P.t("brow")[0:1, 16:16 + HA * SN], in_=din["kern_b"]), "wl", 16)
        P.inc(g.dma_start(out=P.t("outwT")[:], in_=din["outwT"].rearrange("(ch p) n -> p ch n", p=128)), "wl", 16)
    ld_attn_w = P.val("wl")

    emit_ln_tm(P, src=lambda lt: P.t("x0")[:, lt, :], dst=lambda lt: P.t("xh")[:, lt, :],
               wait_sem=("io", ld_x0))
    xh_done = P.val("v")

    emit_dbg_tap(P, dbg, "xh", P.t("xh")[:], [128, NLT, C], BF16, ("v", xh_done))
    emit_dbg_tap(P, dbg, "stat", P.t("stat")[:], [128, 96], F32, ("v", xh_done))

    prev_silu = None
    prev_fp = None
    for lt in range(NLT):
        b = lt % 2

        @block.tensor
        def _(t, lt=lt, b=b):
            if lt == 0:
                P.wait(t, "v", xh_done)
            for ch in range(8):
                bank = pbf(ch // 4)
                col = (ch % 4) * 128
                inst = t.transpose(bank[:, col:col + 128], P.t("xh")[:, lt, ch * 128:(ch + 1) * 128],
                                   P.t("ident_bf")[:])
            P.inc(inst, "pe")
        tr_done = P.val("pe")

        @block.vector
        def _(v, b=b, tr_done=tr_done):
            P.wait(v, "pe", tr_done)
            v.tensor_copy(P.t("xfm_scr")[:, b, 0:4, :].rearrange("p a c -> p (a c)"), pbf(0)[:, 0:512])
            P.inc(v.tensor_copy(P.t("xfm_scr")[:, b, 4:8, :].rearrange("p a c -> p (a c)"), pbf(1)[:, 0:512]), "v")
        evac_done = P.val("v")

        @block.tensor
        def _(t, lt=lt, b=b, evac_done=evac_done, prev_silu=prev_silu):
            P.wait(t, "v", evac_done)
            if lt == 0:
                P.wait(t, "wl", ld_attn_w)
            if prev_silu is not None:
                P.wait(t, "s", prev_silu)
            for ch in range(8):
                lhsT = P.t("xfm_scr")[:, b, ch, :]
                t.matmul(psum[2][:, 0:16], lhsT, P.t("wavewT")[:, ch, :], start=(ch == 0), stop=False)
            t.matmul(psum[2][:, 0:16], P.t("ones_row")[:], P.t("brow")[0:1, 0:16], start=False, stop=True)
            for ch in range(8):
                lhsT = P.t("xfm_scr")[:, b, ch, :]
                t.matmul(psum[3][:, 0:HA * SN], lhsT, P.t("kernwT")[:, ch, :], start=(ch == 0), stop=False)
            P.inc(t.matmul(psum[3][:, 0:HA * SN], P.t("ones_row")[:], P.t("brow")[0:1, 16:16 + HA * SN],
                           start=False, stop=True), "pe")
        mm_done = P.val("pe")

        @block.scalar
        def _(s, lt=lt, mm_done=mm_done, prev_fp=prev_fp):
            P.wait(s, "pe", mm_done)
            if prev_fp is not None:
                P.wait(s, "v", prev_fp)
            s.activation(out=P.t("wavescr")[:, 0:16], in_=psum[2][:, 0:16], func=AF.Silu)
            P.inc(s.activation(out=P.t("wtm")[:, lt, :], in_=psum[3][:, 0:HA * SN], func=AF.Silu), "s")
            s.activation(out=P.t("wavescr")[:, 0:8], in_=P.t("wavescr")[:, 0:8], func=AF.Sigmoid)
            P.inc(s.activation(out=P.t("wavescr")[:, 8:16], in_=P.t("wavescr")[:, 8:16], func=AF.Tanh), "s")
        prev_silu = P.val("s")

        @block.vector
        def _(v, lt=lt, prev_silu=prev_silu):
            P.wait(v, "s", prev_silu)
            v.tensor_reduce(out=P.t("fa")[:, lt:lt + 1], in_=P.t("wavescr")[:, 0:8], axis=AX.X, op=ALU.add)
            P.inc(v.tensor_reduce(out=P.t("pa")[:, lt:lt + 1], in_=P.t("wavescr")[:, 8:16],
                                  axis=AX.X, op=ALU.add), "v")
        prev_fp = P.val("v")

    @block.vector
    def _(v):
        P.wait(v, "v", P.val("v"))
        v.tensor_scalar(out=P.t("fa")[:, 0:NLT], in0=P.t("fa")[:, 0:NLT],
                        scalar1=15.0 / 8.0, scalar2=1.0, op0=ALU.mult, op1=ALU.add)
        v.tensor_scalar(out=P.t("pa")[:, 0:NLT], in0=P.t("pa")[:, 0:NLT],
                        scalar1=16.0 / 8.0, scalar2=0.0, op0=ALU.mult, op1=ALU.add)
        v.nop(cycle_cnt=64)
        for lt in range(NLT):
            v.tensor_scalar(out=P.t("upp")[:, lt, :], in0=P.t("srow")[:], scalar1=P.t("fa")[:, lt:lt + 1],
                            scalar2=P.t("pa")[:, lt:lt + 1], op0=ALU.mult, op1=ALU.add)
        for lt in range(NLT):
            v.tensor_scalar(out=P.t("upp")[:, lt, :], in0=P.t("upp")[:, lt, :],
                            scalar1=P.t("lcol")[:, lt:lt + 1], scalar2=2047.0, op0=ALU.add, op1=ALU.min)
        for lt in range(NLT):
            v.tensor_scalar(out=P.t("upp")[:, lt, :], in0=P.t("upp")[:, lt, :],
                            scalar1=0.0, scalar2=0.0, op0=ALU.max, op1=ALU.add)
        for lt in range(NLT):
            P.inc(v.tensor_scalar(out=P.t("upp")[:, lt, :], in0=P.t("upp")[:, lt, :],
                                  scalar1=P.t("lcol")[:, lt:lt + 1], scalar2=0.0,
                                  op0=ALU.subtract, op1=ALU.add), "v")
    prep_done = P.val("v")

    emit_dbg_tap(P, dbg, "wtm", P.t("wtm")[:], [128, NLT, HA * SN], F32, ("s", P.val("s")))
    emit_dbg_tap(P, dbg, "upp", P.t("upp")[:], [128, NLT, SN], F32, ("v", prep_done))

    hscr = P.alloc("hscr", [128, 2, 276], F32)

    zt = P.t("scr4k")[:].bitcast(BF16)

    @block.vector
    def _(v):
        P.inc(v.memset(P.t("scr4k")[:], 0.0), "v")
    z_done = P.val("v")

    @block.scalar
    def _(s):
        P.wait(s, "v", z_done)
        for k in range(64):
            P.inc(s.dma_start(out=ad_hbm.rearrange("(p f) -> p f", p=128)[:, k * 2048:(k + 1) * 2048],
                              in_=zt), "zz", 16)
    zero_hbm = P.val("zz")

    prev_fma = {}
    adz_done = {}
    for lt in range(NLT):
        sub = lt % 2

        @block.scalar
        def _(s, lt=lt):
            if lt == 0:
                P.wait(s, "v", prep_done)
            else:
                P.wait(s, "v", prev_fma[lt - 1])
            for sv in range(SN):
                lo_m, hi_m = BANDS[sv]
                w = hi_m - lo_m + 1
                s.activation(out=P.t("hscr")[:, sv % 2, 0:w],
                             in_=P.t("iota_ext")[:, 304 + lo_m:304 + lo_m + w],
                             func=AF.Abs, scale=-1.0, bias=P.t("upp")[:, lt, sv:sv + 1])
                if sv > 0:
                    lo_p, hi_p = BANDS[sv - 1]
                    wp = hi_p - lo_p + 1
                    P.inc(s.activation(out=P.t("hats")[:, HOFF[sv - 1]:HOFF[sv - 1] + wp],
                                       in_=P.t("hscr")[:, (sv - 1) % 2, 0:wp],
                                       func=AF.Relu, scale=-1.0, bias=1.0), "hs")
            lo_p, hi_p = BANDS[SN - 1]
            wp = hi_p - lo_p + 1
            P.inc(s.activation(out=P.t("hats")[:, HOFF[SN - 1]:HOFF[SN - 1] + wp],
                               in_=P.t("hscr")[:, (SN - 1) % 2, 0:wp],
                               func=AF.Relu, scale=-1.0, bias=1.0), "hs")
        hs_done = P.val("hs")

        if lt < 2:
            @block.gpsimd
            def _(g, lt=lt, sub=sub):
                if lt >= 1:
                    P.wait(g, "v")
                P.inc(g.memset(P.t("ad2")[:, sub, :, :], 0.0), "adz")
            adz_done[lt] = P.val("adz")

        @block.vector
        def _(v, lt=lt, sub=sub, hs_done=hs_done, adz_done=adz_done[lt]):
            P.wait(v, "adz", adz_done)
            base = hs_done - SN
            for sv in range(SN):
                lo_m, hi_m = BANDS[sv]
                w = hi_m - lo_m + 1
                P.wait(v, "hs", base + sv + 1)
                for h in range(HA):
                    ad = P.t("ad2")[:, sub, h, lo_m + 273:lo_m + 273 + w]
                    inst = v.scalar_tensor_tensor(out=ad, in0=P.t("hats")[:, HOFF[sv]:HOFF[sv] + w],
                                                  scalar=P.t("wtm")[:, lt, h * SN + sv:h * SN + sv + 1],
                                                  in1=ad, op0=ALU.mult, op1=ALU.add)
            P.inc(inst, "v")
        fma_done = P.val("v")
        prev_fma[lt] = fma_done

        @block.gpsimd
        def _(g, lt=lt, sub=sub, fma_done=fma_done):
            P.wait(g, "v", fma_done)
            if lt == 0:
                P.wait(g, "zz", zero_hbm)
            dst = bass.AP(ad_hbm.tensor, lt * S_LT + AD_OFF, [[S_P, 128], [S_H, HA], [1, ADW]])
            P.inc(g.dma_start(out=dst, in_=P.t("ad2")[:, sub, :, :]), "adw", 16)
            if lt + 2 < NLT:
                P.wait(g, "adw", 16 * (lt + 1))
                P.inc(g.memset(P.t("ad2")[:, sub, :, :], 0.0), "adz")
        if lt + 2 < NLT:
            adz_done[lt + 2] = P.val("adz")
    adw_all = P.val("adw")

    tp_hist = []
    for lt in range(NLT):
        b = lt % 2
        cvalid = [c for c in range(ACH) if 0 <= lt - 3 + c <= 15]

        for h in range(HA):
            hb = h % 2

            @block.sync
            def _(sync, lt=lt, h=h, hb=hb):
                if h == 0:
                    P.wait(sync, "adw", 16 * (lt + 1))
                if len(tp_hist) >= 2:
                    P.wait(sync, "pe", tp_hist[-2])
                else:
                    P.wait(sync, "pe")
                src = bass.AP(ad_hbm.tensor, lt * S_LT + h * S_H + RD_C0, [[S_P - 1, 128], [1, ABW]])
                P.inc(sync.dma_start(out=P.t("aabs")[:, hb, :], in_=src), "shr", 16)
            shr_h = P.val("shr")

            @block.tensor
            def _(t, lt=lt, h=h, hb=hb, shr_h=shr_h, cvalid=cvalid):
                P.wait(t, "shr", shr_h)
                if lt > 0 or h > 1:
                    P.wait(t, "v")
                for c in cvalid:
                    inst = t.transpose(pbf(4)[:, c * 128:(c + 1) * 128],
                                       P.t("aabs")[:, hb, c * 128:(c + 1) * 128], P.t("ident_bf")[:])
                P.inc(inst, "pe")
            tp_done = P.val("pe")
            tp_hist.append(tp_done)

            @block.vector
            def _(v, lt=lt, hb=hb, tp_done=tp_done, cvalid=cvalid):
                P.wait(v, "pe", tp_done)
                c0, c1 = cvalid[0], cvalid[-1]
                P.inc(v.tensor_copy(P.t("akm")[:, hb, c0 * 128:(c1 + 1) * 128],
                                    pbf(4)[:, c0 * 128:(c1 + 1) * 128]), "v")
            km_done = P.val("v")

            @block.tensor
            def _(t, lt=lt, h=h, hb=hb, km_done=km_done, cvalid=cvalid):
                P.wait(t, "v", km_done)
                bank = 0 if h < 4 else 1
                col = (h % 4) * 128
                for i, c in enumerate(cvalid):
                    inst = t.matmul(psum[bank][:, col:col + 128],
                                    P.t("xh")[:, lt - 3 + c, h * 128:(h + 1) * 128],
                                    P.t("akm")[:, hb, c * 128:(c + 1) * 128],
                                    start=(i == 0), stop=(i == len(cvalid) - 1))
                P.inc(inst, "pe")
        ein_done = P.val("pe")

        @block.vector
        def _(v, lt=lt, b=b, ein_done=ein_done):
            P.wait(v, "pe", ein_done)
            v.tensor_copy(P.t("ohfm")[:, b, 0:512], psum[0][:])
            P.inc(v.tensor_copy(P.t("ohfm")[:, b, 512:1024], psum[1][:]), "v")
        oh_done = P.val("v")

        @block.tensor
        def _(t, lt=lt, b=b, oh_done=oh_done):
            P.wait(t, "v", oh_done)
            P.wait(t, "s")
            for n2 in range(2):
                for hh in range(HA):
                    inst = t.matmul(psum[2 + n2][:, 0:512],
                                    P.t("ohfm")[:, b, hh * 128:(hh + 1) * 128],
                                    P.t("outwT")[:, hh, n2 * 512:(n2 + 1) * 512],
                                    start=(hh == 0), stop=(hh == HA - 1))
            P.inc(inst, "pe")
        ow_done = P.val("pe")

        @block.scalar
        def _(s, lt=lt, ow_done=ow_done):
            P.wait(s, "pe", ow_done)
            P.wait(s, "v")
            s.activation(out=P.t("scr4k")[:, 0:512], in_=psum[2][:], func=AF.Silu)
            P.inc(s.activation(out=P.t("scr4k")[:, 512:1024], in_=psum[3][:], func=AF.Silu), "s")
        sil_done = P.val("s")

        @block.vector
        def _(v, lt=lt, sil_done=sil_done):
            P.wait(v, "s", sil_done)
            P.inc(v.tensor_tensor(out=P.t("x0")[:, lt, :], in0=P.t("x0")[:, lt, :],
                                  in1=P.t("scr4k")[:], op=ALU.add), "v")
    x1_done = P.val("v")

    emit_dbg_tap(P, dbg, "x1", P.t("x0")[:], [128, NLT, C], F32, ("v", x1_done))

    P.reset_to("after_x0")
    zf = P.alloc("zf", [128, 8, 2052], F32)
    zcast = P.alloc("zcast", [128, 8, 128], BF16)
    mu_bf = P.alloc("mu_bf", [1, 2048], BF16)
    sig_bf = P.alloc("sig_bf", [1, 2048], BF16)
    invcol = P.alloc("invcol", [128, 16], F32)
    rz = P.alloc("rz", [128, 32], F32)
    qscol = P.alloc("qscol", [64, 4], F32)
    lgcol = P.alloc("lgcol", [128, 8], F32)
    lbcol = P.alloc("lbcol", [128, 8], BF16)
    b2col = P.alloc("b2col", [128, 8], F32)
    browf = P.alloc("browf", [1, 1024], F32)
    P.mark("ffn_dyn")
    vwin = P.alloc("vwin", [128, 5, INNER], BF16)
    kfm = P.alloc("kfm", [128, 4, 5, 128], BF16)
    qfm = P.alloc("qfm", [128, 4, 128], BF16)
    ofm = P.alloc("ofm", [128, 3, 4, 128], BF16)
    tabt = P.alloc("tabt", [128, 4, 64], BF16)
    tabscr = P.alloc("tabscr", [64, 4, 128], BF16)
    tabst = P.alloc("tabst", [64, 4, 128], BF16)
    qscr = P.alloc("qscr", [128, 2, INNER], F32)
    qrot = P.alloc("qrot", [128, 2, INNER], BF16)
    rotscr = P.alloc("rotscr", [128, 2, INNER], BF16)
    qkt = P.alloc("qkt", [128, 2, INNER], BF16)
    ptm = P.alloc("ptm", [128, 2, 640], BF16)
    pkm = P.alloc("pkm", [128, 2, 640], BF16)
    otm = P.alloc("otm", [128, INNER], BF16)
    attn_grp_end = P.sbuf_off
    P.reset_to("ffn_dyn")
    h4k = P.alloc("h4k", [128, NLT, SLW + 1], BF16)
    h4fmk = P.alloc("h4fmk", [128, 4, 1024], BF16)
    ggscr = P.alloc("ggscr", [128, SLW + 1], BF16)
    crow = P.alloc("crow", [1, 4, SLW + 1], BF16)
    nbrow = P.alloc("nbrow", [1, 4, SLW + 1], BF16)
    stB = P.alloc("stB", [128, 2, 2, 512], BF16)
    mlp_grp_end = P.sbuf_off
    P.sbuf_off = max(attn_grp_end, mlp_grp_end)
    x1t = P.alloc("x1t", [128, 2, C], F32)
    wq_off = P.marks["persist"]
    sv_off = P.sbuf_off
    P.sbuf_off = wq_off
    wqkv_sb = P.alloc("wqkv_sb", [128, 8, 3 * INNER], BF16)
    wo_sb = P.alloc("wo_sb", [128, 4, C], BF16)
    w1s = P.alloc("w1s", [128, 4, 8, SLW], BF16)
    assert P.sbuf_off <= P.marks["after_x0"], (P.sbuf_off, P.marks["after_x0"])
    w2_off = wq_off
    P.sbuf_off = w2_off
    w2s = P.alloc("w2s", [128, 2, 4, C], BF16)
    assert P.sbuf_off <= P.marks["after_x0"]
    P.sbuf_off = sv_off
    assert P.sbuf_peak < 229000, P.sbuf_peak

    emit_ln_tm(P, src=lambda lt: P.t("x0")[:, lt, :], dst=None, wait_sem=("v", x1_done))
    ln2_stats = P.val("s")

    @block.vector
    def _(v):
        P.inc(v.memset(zf[:, :, 0:1], 0.0), "v")

    @block.sync
    def _(sync):
        P.wait(sync, "v", x1_done)
        P.inc(sync.dma_start(out=x1_spill.rearrange("(lt p) c -> p lt c", p=128), in_=P.t("x0")[:]), "sp", 16)
    spill_done = P.val("sp")

    st = P.t("stat")
    for lt in range(NLT):
        @block.vector
        def _(v, lt=lt):
            if lt == 0:
                P.wait(v, "s", ln2_stats)
            P.wait(v, "pe")
            P.inc(v.scalar_tensor_tensor(out=P.t("scr4k")[:], in0=P.t("x0")[:, lt, :],
                                         scalar=st[:, lt:lt + 1], in1=bcol(st, 16 + lt, C),
                                         op0=ALU.add, op1=ALU.mult), "v")
        ap_done = P.val("v")

        @block.tensor
        def _(t, lt=lt, ap_done=ap_done):
            P.wait(t, "v", ap_done)
            for ch in range(8):
                bank = 0 if ch < 4 else 1
                col = (ch % 4) * 128
                inst = t.transpose(psum[bank][:, col:col + 128],
                                   P.t("scr4k")[:, ch * 128:(ch + 1) * 128],
                                   P.t("ident_f")[:])
            P.inc(inst, "pe")
        tp_done = P.val("pe")

        @block.vector
        def _(v, lt=lt, tp_done=tp_done):
            P.wait(v, "pe", tp_done)
            v.tensor_copy(zf[:, 0:4, 1 + lt * 128:1 + lt * 128 + 128], psum[0][:])
            P.inc(v.tensor_copy(zf[:, 4:8, 1 + lt * 128:1 + lt * 128 + 128], psum[1][:]), "v")
    z0_done = P.val("v")

    emit_dbg_tap(P, dbg, "z0", zf[:], [128, 8, 2052], F32, ("v", z0_done))

    for ly in range(DEPTH):
        emit_ffn_layer(P, ly, din, zf, psum, pbf, dbg, con, h4_hbm)

    zfin = P.val("v")

    @block.sync
    def _(sync):
        P.wait(sync, "sp", spill_done)
        P.wait(sync, "v", zfin)
        for b2 in range(2):
            P.inc(sync.dma_start(out=x1t[:, b2, :],
                                 in_=x1_spill.rearrange("(lt p) c -> p lt c", p=128)[:, b2, :]), "x1l", 16)

    for lt in range(NLT):
        b = lt % 2

        @block.tensor
        def _(t, lt=lt, zfin=zfin):
            if lt == 0:
                P.wait(t, "v", zfin)
                P.wait(t, "s")
            else:
                P.wait(t, "v")
            for ch in range(8):
                bank = 0 if ch < 4 else 1
                col = (ch % 4) * 128
                inst = t.transpose(psum[bank][:, col:col + 128],
                                   zf[:, ch, 1 + lt * 128:1 + lt * 128 + 128], P.t("ident_f")[:])
            P.inc(inst, "pe")
        g_tp = P.val("pe")

        @block.vector
        def _(v, lt=lt, b=b, g_tp=g_tp):
            P.wait(v, "pe", g_tp)
            P.wait(v, "x1l", 16 * (lt + 1))
            P.wait(v, "st")
            v.tensor_tensor(out=P.t("scr4k")[:, 0:512], in0=psum[0][:], in1=x1t[:, b, 0:512], op=ALU.add)
            P.inc(v.tensor_tensor(out=P.t("scr4k")[:, 512:1024], in0=psum[1][:], in1=x1t[:, b, 512:1024],
                                  op=ALU.add), "v")
        g_add = P.val("v")

        @block.sync
        def _(sync, lt=lt, b=b, g_add=g_add):
            P.wait(sync, "v", g_add)
            P.inc(sync.dma_start(out=y_out.rearrange("(lt p) c -> p lt c", p=128)[:, lt, :],
                                 in_=P.t("scr4k")[:, 0:1024]), "st", 16)
            if lt + 2 < NLT:
                P.inc(sync.dma_start(out=x1t[:, b, :],
                                     in_=x1_spill.rearrange("(lt p) c -> p lt c", p=128)[:, lt + 2, :]),
                      "x1l", 16)

    @block.sync
    def _(sync):
        sync.wait_ge(P.sem("st"), P.val("st"))



def emit_ffn_layer(P, ly, din, zf, psum, pbf, dbg, con, h4_hbm):
    block = P.block
    st = P.t("stat")
    scr = P.t("scr4k")

    @block.gpsimd
    def _(g):
        P.wait(g, "v")
        P.wait(g, "pe")
        P.inc(g.dma_start(out=P.t("wqkv_sb")[:], in_=din["wqkv"][ly].rearrange("(ch p) n -> p ch n", p=128)), "wl", 16)
        P.inc(g.dma_start(out=P.t("wo_sb")[:], in_=din["wo"][ly].rearrange("(ic p) n -> p ic n", p=128)), "wl", 16)
        P.inc(g.dma_start(out=P.t("qscol")[:, 0:1], in_=din["qs"][ly]), "wl", 16)
        P.inc(g.dma_start(out=P.t("qscol")[:, 2:3], in_=din["ks"][ly]), "wl", 16)
        P.inc(g.dma_start(out=P.t("lgcol")[:], in_=din["ln_g"][ly]), "wl", 16)
        P.inc(g.dma_start(out=P.t("lbcol")[:], in_=din["ln_b"][ly]), "wl", 16)
        P.inc(g.dma_start(out=P.t("b2col")[:], in_=din["b2"][ly]), "wl", 16)
    wl_base = P.val("wl")

    @block.vector
    def _(v):
        P.wait(v, "wl", wl_base)
        v.tensor_copy(P.t("qscol")[0:32, 1:2], P.t("qscol")[32:64, 0:1])
        v.tensor_copy(P.t("qscol")[0:32, 3:4], P.t("qscol")[32:64, 2:3])
        v.tensor_copy(P.t("qscol")[32:64, 1:2], P.t("qscol")[0:32, 0:1])
        P.inc(v.tensor_copy(P.t("qscol")[32:64, 3:4], P.t("qscol")[0:32, 2:3]), "v")
    qsw_done = P.val("v")

    for lt in range(NLT):
        l0 = lt * 128
        j0t = max(0, lt - 4)
        nkt = lt - j0t + 1

        @block.sync
        def _(sync, lt=lt, l0=l0):
            if lt == 0:
                P.wait(sync, "wl", wl_base)
            else:
                P.wait(sync, "v")
            P.inc(sync.dma_start(out=P.t("tabst")[:, 0:2, :],
                                 in_=con["tabA"].ap().rearrange("q d l -> d q l")[:, :, l0:l0 + 128]), "tb", 16)
            P.inc(sync.dma_start(out=P.t("tabst")[:, 2:4, :],
                                 in_=con["tabB"].ap().rearrange("q d l -> d q l")[:, :, l0:l0 + 128]), "tb", 16)
        tb_done = P.val("tb")

        @block.vector
        def _(v, lt=lt, tb_done=tb_done, qsw_done=qsw_done):
            P.wait(v, "tb", tb_done)
            if lt == 0:
                P.wait(v, "v", qsw_done)
            else:
                P.wait(v, "v")
            for i, (src_i, scol) in enumerate([(0, 0), (2, 1), (1, 2), (3, 3)]):
                inst = v.tensor_scalar(out=P.t("tabscr")[:, i, :], in0=P.t("tabst")[:, src_i, :],
                                       scalar1=P.t("qscol")[:, scol:scol + 1], scalar2=0.0,
                                       op0=ALU.mult, op1=ALU.add)
            P.inc(inst, "v")
        tsc_done = P.val("v")

        @block.vector
        def _(v, lt=lt, l0=l0):
            P.wait(v, "pe")
            v.tensor_copy(P.t("zcast")[:, 0:4, :], zf[:, 0:4, 1 + l0:1 + l0 + 128])
            P.inc(v.tensor_copy(P.t("zcast")[:, 4:8, :], zf[:, 4:8, l0:l0 + 128]), "v")
        zc_done = P.val("v")

        @block.tensor
        def _(t, lt=lt, zc_done=zc_done):
            P.wait(t, "v", zc_done)
            P.wait(t, "s")
            P.wait(t, "v")
            if lt == 0:
                P.wait(t, "wl", wl_base)
            for n in range(3):
                for ch in range(8):
                    inst = t.matmul(psum[n][:, 0:512], P.t("zcast")[:, ch, :],
                                    P.t("wqkv_sb")[:, ch, n * 512:(n + 1) * 512],
                                    start=(ch == 0), stop=(ch == 7))
            P.inc(inst, "pe")
        qkv_done = P.val("pe")

        @block.scalar
        def _(s, lt=lt, qkv_done=qkv_done):
            P.wait(s, "pe", qkv_done)
            P.wait(s, "v")
            s.activation(out=P.t("qscr")[:, 0, :], in_=psum[0][:], func=AF.Copy)
            P.inc(s.activation(out=P.t("qscr")[:, 1, :], in_=psum[1][:], func=AF.Copy), "s")
            s.activation(out=scr[:, 0:512], in_=psum[0][:], func=AF.Square)
            P.inc(s.activation(out=scr[:, 512:1024], in_=psum[1][:], func=AF.Square), "s")
        qk_done = P.val("s")

        @block.vector
        def _(v, lt=lt, qkv_done=qkv_done, qk_done=qk_done):
            P.wait(v, "pe", qkv_done)
            P.inc(v.tensor_copy(P.t("vwin")[:, lt % 5, :], psum[2][:]), "v")
            P.wait(v, "s", qk_done)
            v.tensor_reduce(out=P.t("rz")[:, 16:24], in_=scr[:, 0:512].rearrange("p (h d) -> p h d", h=8),
                            axis=AX.X, op=ALU.add)
            P.inc(v.tensor_reduce(out=P.t("rz")[:, 24:32], in_=scr[:, 512:1024].rearrange("p (h d) -> p h d", h=8),
                                  axis=AX.X, op=ALU.add), "v")
        ss_done = P.val("v")

        @block.scalar
        def _(s, lt=lt, ss_done=ss_done):
            P.wait(s, "v", ss_done)
            s.activation(out=st[:, 80:88], in_=P.t("rz")[:, 16:24], func=AF.Relu,
                         bias=P.t("epscol")[:, 0:1])
            s.activation(out=st[:, 88:96], in_=P.t("rz")[:, 24:32], func=AF.Relu,
                         bias=P.t("epscol")[:, 0:1])
            s.activation(out=st[:, 80:88], in_=st[:, 80:88], func=AF.Ln, bias=P.t("epscol")[:, 1:2])
            s.activation(out=st[:, 88:96], in_=st[:, 88:96], func=AF.Ln, bias=P.t("epscol")[:, 1:2])
            s.activation(out=st[:, 80:88], in_=st[:, 80:88], func=AF.Exp, scale=-0.5)
            P.inc(s.activation(out=st[:, 88:96], in_=st[:, 88:96], func=AF.Exp, scale=-0.5), "s")
        rn_done = P.val("s")

        @block.tensor
        def _(t, lt=lt, tsc_done=tsc_done):
            P.wait(t, "v", tsc_done)
            for i in range(4):
                inst = t.transpose(pbf(3)[:, i * 64:(i + 1) * 64], P.t("tabscr")[:, i, :],
                                   P.t("ident_bf")[0:64, 0:64])
            P.inc(inst, "pe")
        ttp_done = P.val("pe")

        @block.vector
        def _(v, lt=lt, ttp_done=ttp_done):
            P.wait(v, "pe", ttp_done)
            P.inc(v.tensor_copy(P.t("tabt")[:].rearrange("p a d -> p (a d)"), pbf(3)[:, 0:256]), "v")
        tabt_done = P.val("v")

        @block.vector
        def _(v, lt=lt, rn_done=rn_done, tabt_done=tabt_done):
            P.wait(v, "s", rn_done)
            P.wait(v, "v", tabt_done)
            rnq = bass.AP(st, 80, [[96, 128], [1, 8], [0, 64]])
            rnk = bass.AP(st, 88, [[96, 128], [1, 8], [0, 64]])
            v.scalar_tensor_tensor(out=P.t("qrot")[:, 0, :], in0=P.t("qscr")[:, 0, :], scalar=1.0,
                                   in1=rnq, op0=ALU.mult, op1=ALU.mult)
            v.scalar_tensor_tensor(out=P.t("qrot")[:, 1, :], in0=P.t("qscr")[:, 1, :], scalar=1.0,
                                   in1=rnk, op0=ALU.mult, op1=ALU.mult)
            tA_q = bass.AP(P.t("tabt"), 0, [[256, 128], [0, 8], [1, 64]])
            tB_q = bass.AP(P.t("tabt"), 64, [[256, 128], [0, 8], [1, 64]])
            tA_k = bass.AP(P.t("tabt"), 128, [[256, 128], [0, 8], [1, 64]])
            tB_k = bass.AP(P.t("tabt"), 192, [[256, 128], [0, 8], [1, 64]])
            v.scalar_tensor_tensor(out=P.t("qkt")[:, 0, :], in0=P.t("qrot")[:, 0, :], scalar=1.0,
                                   in1=tA_q, op0=ALU.mult, op1=ALU.mult)
            v.scalar_tensor_tensor(out=P.t("qkt")[:, 1, :], in0=P.t("qrot")[:, 1, :], scalar=1.0,
                                   in1=tA_k, op0=ALU.mult, op1=ALU.mult)
            def h32(t, off):
                return bass.AP(t, off, [[int(np.prod(t.shape[1:])), 128], [64, 8], [1, 32]])
            def tb32(off):
                return bass.AP(P.t("tabt"), off, [[256, 128], [0, 8], [1, 32]])
            qr, rs = P.t("qrot"), P.t("rotscr")
            v.scalar_tensor_tensor(out=h32(rs, 0), in0=h32(qr, 32), scalar=1.0,
                                   in1=tb32(64), op0=ALU.mult, op1=ALU.mult)
            v.scalar_tensor_tensor(out=h32(rs, 512), in0=h32(qr, 512 + 32), scalar=1.0,
                                   in1=tb32(192), op0=ALU.mult, op1=ALU.mult)
            v.scalar_tensor_tensor(out=h32(rs, 32), in0=h32(qr, 0), scalar=1.0,
                                   in1=tb32(64 + 32), op0=ALU.mult, op1=ALU.mult)
            v.scalar_tensor_tensor(out=h32(rs, 512 + 32), in0=h32(qr, 512), scalar=1.0,
                                   in1=tb32(192 + 32), op0=ALU.mult, op1=ALU.mult)
            v.tensor_tensor(out=P.t("qkt")[:, 0, :], in0=P.t("qkt")[:, 0, :],
                            in1=P.t("rotscr")[:, 0, :], op=ALU.add)
            P.inc(v.tensor_tensor(out=P.t("qkt")[:, 1, :], in0=P.t("qkt")[:, 1, :],
                                  in1=P.t("rotscr")[:, 1, :], op=ALU.add), "v")
        rot_done = P.val("v")

        d5 = dbg(f"snap{ly}", [128, 2, 1024], F32) if lt == 5 else None
        if d5 is not None:
            @block.vector
            def _(v, d5=d5):
                v.tensor_copy(P.t("x1t")[:, 0, :], P.t("qkt")[:].rearrange("p a d -> p (a d)"))
                P.inc(v.tensor_copy(P.t("x1t")[:, 1, :], P.t("qscr")[:].rearrange("p a d -> p (a d)")), "v")
            dbg.cache[f"snapreq{ly}"] = d5

        @block.tensor
        def _(t, lt=lt, rot_done=rot_done):
            P.wait(t, "v", rot_done)
            for hp in range(4):
                t.transpose(pbf(3)[:, hp * 128:(hp + 1) * 128],
                            P.t("qkt")[:, 0, hp * 128:(hp + 1) * 128], P.t("ident_bf")[:])
            for hp in range(4):
                inst = t.transpose(pbf(3)[:, 512 + hp * 128:512 + (hp + 1) * 128],
                                   P.t("qkt")[:, 1, hp * 128:(hp + 1) * 128], P.t("ident_bf")[:])
            P.inc(inst, "pe")
        qktp_done = P.val("pe")

        @block.vector
        def _(v, lt=lt, qktp_done=qktp_done):
            P.wait(v, "pe", qktp_done)
            v.tensor_copy(P.t("qfm")[:].rearrange("p a d -> p (a d)"), pbf(3)[:, 0:512])
            P.inc(v.tensor_copy(P.t("kfm")[:, :, lt % 5, :], pbf(3)[:, 512:1024]), "v")
        qkfm_done = P.val("v")

        for h in range(HF):
            hp, hh = h // 2, h % 2
            prow = slice(64 * hh, 64 * (hh + 1))

            @block.tensor
            def _(t, lt=lt, h=h, hp=hp, prow=prow, j0t=j0t, nkt=nkt, qkfm_done=qkfm_done):
                P.wait(t, "v", qkfm_done)
                P.wait(t, "s")
                if h == 0:
                    P.wait(t, "v")
                for i in range(nkt):
                    kt = j0t + i
                    bank, col = (5, i * 128) if i < 4 else (6, 0)
                    inst = t.matmul(psum[bank][:, col:col + 128],
                                    P.t("qfm")[prow, hp, :],
                                    P.t("kfm")[prow, hp, kt % 5, :],
                                    start=True, stop=True)
                P.inc(inst, "pe")
            sc_done = P.val("pe")

            @block.vector
            def _(v, lt=lt, h=h, nkt=nkt, sc_done=sc_done):
                P.wait(v, "pe", sc_done)
                tb_, tc_ = (5, (nkt - 1) * 128) if nkt - 1 < 4 else (6, 0)
                v.tensor_tensor(out=psum[tb_][:, tc_:tc_ + 128], in0=psum[tb_][:, tc_:tc_ + 128],
                                in1=P.t("mask_tail")[:], op=ALU.add)
                if lt >= 4:
                    v.tensor_tensor(out=psum[5][:, 0:128], in0=psum[5][:, 0:128],
                                    in1=P.t("mask_lead")[:], op=ALU.add)
                v.nop(cycle_cnt=64)
                P.inc(v.tensor_reduce(out=P.t("rz")[:, 10:11], in_=P.pS[:, 0:nkt * 128], axis=AX.X,
                                      op=ALU.max, negate=True), "v")
            msk_done = P.val("v")

            @block.scalar
            def _(s, lt=lt, h=h, nkt=nkt, msk_done=msk_done):
                P.wait(s, "v", msk_done)
                w5 = min(nkt, 4) * 128
                s.activation(out=P.t("ptm")[:, h % 2, 0:w5], in_=psum[5][:, 0:w5], func=AF.Exp,
                             bias=P.t("rz")[:, 10:11], accum_out=P.t("rz")[:, 11:12])
                if nkt == 5:
                    P.inc(s.activation(out=P.t("ptm")[:, h % 2, 512:640], in_=psum[6][:, 0:128],
                                       func=AF.Exp, bias=P.t("rz")[:, 10:11],
                                       accum_out=P.t("rz")[:, 12:13]), "s")
                else:
                    P.inc(s.activation(out=P.t("rz")[:, 12:13], in_=P.t("rz")[:, 11:12],
                                       func=AF.Copy, scale=0.0), "s")
            exp_done = P.val("s")

            d3 = dbg(f"hdbg{ly}", [128, NLT, HF, 8], F32) if (lt == 0 and h == 0) else dbg.cache.get(f"hdbg{ly}")
            if d3 is not None:
                @block.sync
                def _(sync, lt=lt, h=h, d3=d3, exp_done=exp_done):
                    P.wait(sync, "s", exp_done)
                    P.inc(sync.dma_start(out=d3[:, lt, h, :], in_=P.t("rz")[:, 8:16]), "st", 16)
            d4 = dbg(f"pdbg{ly}", [128, HF, 640], BF16) if (lt == 5 and h == 0) else (dbg.cache.get(f"pdbg{ly}") if lt == 5 else None)
            if d4 is not None:
                @block.sync
                def _(sync, h=h, d4=d4, exp_done=exp_done):
                    P.wait(sync, "s", exp_done)
                    P.inc(sync.dma_start(out=d4[:, h, :], in_=P.t("ptm")[:, h % 2, :]), "st", 16)

            @block.vector
            def _(v, h=h, exp_done=exp_done):
                P.wait(v, "s", exp_done)
                P.inc(v.tensor_tensor(out=P.t("rz")[:, h:h + 1], in0=P.t("rz")[:, 11:12],
                                      in1=P.t("rz")[:, 12:13], op=ALU.add), "v")
            zs_done = P.val("v")

            @block.tensor
            def _(t, lt=lt, h=h, nkt=nkt, zs_done=zs_done):
                P.wait(t, "v", zs_done)
                P.wait(t, "v")
                for i in range(nkt):
                    col = i * 128
                    src = P.t("ptm")[:, h % 2, col:col + 128] if i < 4 else P.t("ptm")[:, h % 2, 512:640]
                    inst = t.transpose(pbf(4)[:, col:col + 128], src, P.t("ident_bf")[:])
                P.inc(inst, "pe")
            ptp_done = P.val("pe")

            @block.vector
            def _(v, h=h, nkt=nkt, ptp_done=ptp_done):
                P.wait(v, "pe", ptp_done)
                P.inc(v.tensor_copy(P.t("pkm")[:, h % 2, 0:nkt * 128], pbf(4)[:, 0:nkt * 128]), "v")
            pkm_done = P.val("v")

            @block.tensor
            def _(t, lt=lt, h=h, j0t=j0t, nkt=nkt, pkm_done=pkm_done):
                P.wait(t, "v", pkm_done)
                for i in range(nkt):
                    kt = j0t + i
                    inst = t.matmul(psum[7][:, h * 64:(h + 1) * 64],
                                    P.t("pkm")[:, h % 2, i * 128:(i + 1) * 128],
                                    P.t("vwin")[:, kt % 5, h * 64:(h + 1) * 64],
                                    start=(i == 0), stop=(i == nkt - 1))
                P.inc(inst, "pe")
        av_done = P.val("pe")

        @block.vector
        def _(v, lt=lt, av_done=av_done):
            P.wait(v, "pe", av_done)
            v.nop(cycle_cnt=32)
            v.reciprocal(out=P.t("rz")[:, 0:8], in_=P.t("rz")[:, 0:8])
            v.nop(cycle_cnt=32)
            rzb = bass.AP(P.t("rz"), 0, [[32, 128], [1, 8], [0, 64]])
            P.inc(v.scalar_tensor_tensor(out=P.t("otm")[:], in0=psum[7][:], scalar=1.0,
                                         in1=rzb, op0=ALU.mult, op1=ALU.mult), "v")
        onorm_done = P.val("v")

        d = dbg(f"otm{ly}", [128, NLT, INNER], BF16) if lt == 0 else dbg.cache.get(f"otm{ly}")
        if d is not None:
            @block.sync
            def _(sync, lt=lt, d=d, onorm_done=onorm_done):
                P.wait(sync, "v", onorm_done)
                P.inc(sync.dma_start(out=d[:, lt, :], in_=P.t("otm")[:]), "st", 16)
        d2 = dbg(f"rza{ly}", [128, NLT, 32], F32) if lt == 0 else dbg.cache.get(f"rza{ly}")
        if d2 is not None:
            @block.sync
            def _(sync, lt=lt, d2=d2, onorm_done=onorm_done):
                P.wait(sync, "v", onorm_done)
                P.inc(sync.dma_start(out=d2[:, lt, :], in_=P.t("rz")[:]), "st", 16)

        @block.tensor
        def _(t, lt=lt, onorm_done=onorm_done):
            P.wait(t, "v", onorm_done)
            P.wait(t, "st")
            for ic in range(4):
                inst = t.transpose(pbf(4)[:, ic * 128:(ic + 1) * 128],
                                   P.t("otm")[:, ic * 128:(ic + 1) * 128], P.t("ident_bf")[:])
            P.inc(inst, "pe")
        otp_done = P.val("pe")

        @block.vector
        def _(v, lt=lt, otp_done=otp_done):
            P.wait(v, "pe", otp_done)
            P.inc(v.tensor_copy(P.t("ofm")[:, lt % 3, :, :].rearrange("p a d -> p (a d)"),
                                pbf(4)[:, 0:512]), "v")

        if lt >= 2:
            emit_wo_update(P, lt - 2, zf, psum)
    emit_wo_update(P, NLT - 2, zf, psum)
    emit_wo_update(P, NLT - 1, zf, psum)
    attn_done = P.val("v")

    emit_dbg_tap(P, dbg, f"zl{ly}a", zf[:], [128, 8, 2052], F32, ("v", attn_done))
    dsnap = dbg.cache.get(f"snapreq{ly}")
    if dsnap is not None:
        @block.sync
        def _(sync, dsnap=dsnap):
            P.wait(sync, "v", attn_done)
            P.inc(sync.dma_start(out=dsnap, in_=P.t("x1t")[:]), "st", 16)
    dsnap2 = dbg(f"mlpsnap{ly}", [128, 2, 1024], F32) if getattr(P, "snap_mlp", None) == ly else None
    P.snap_pending = dsnap2

    emit_ffn_stats(P, zf, psum, pbf)

    for pair in range(NSL):
        emit_w1_pair(P, ly, pair, din, zf, psum, pbf, h4_hbm)
    for pair in range(NSL):
        emit_w2_pair(P, ly, pair, din, zf, psum, pbf, h4_hbm)

    if getattr(P, "snap_pending", None) is not None:
        dsp = P.snap_pending

        @block.sync
        def _(sync, dsp=dsp):
            P.wait(sync, "v", getattr(P, "snapdone", P.val("v")))
            P.inc(sync.dma_start(out=dsp, in_=P.t("x1t")[:]), "st", 16)
        P.snap_pending = None

    emit_dbg_tap(P, dbg, f"zl{ly}", zf[:], [128, 8, 2052], F32, ("v", P.val("v")))


def emit_wo_update(P, lt, zf, psum):
    block = P.block
    l0 = lt * 128

    @block.tensor
    def _(t, lt=lt):
        P.wait(t, "v")
        P.wait(t, "s")
        for half in range(2):
            bank = 5 if half == 0 else 6
            for cq in range(4):
                ch = half * 4 + cq
                for ic in range(4):
                    inst = t.matmul(psum[bank][:, cq * 128:(cq + 1) * 128],
                                    P.t("wo_sb")[:, ic, ch * 128:(ch + 1) * 128],
                                    P.t("ofm")[:, lt % 3, ic, :],
                                    start=(ic == 0), stop=(ic == 3))
        P.inc(inst, "pe")
    wo_done = P.val("pe")

    @block.vector
    def _(v, lt=lt, l0=l0, wo_done=wo_done):
        P.wait(v, "pe", wo_done)
        v.tensor_tensor(out=zf[:, 0:4, 1 + l0:1 + l0 + 128], in0=zf[:, 0:4, 1 + l0:1 + l0 + 128],
                        in1=psum[5][:].rearrange("p (a d) -> p a d", a=4), op=ALU.add)
        P.inc(v.tensor_tensor(out=zf[:, 4:8, 1 + l0:1 + l0 + 128], in0=zf[:, 4:8, 1 + l0:1 + l0 + 128],
                              in1=psum[6][:].rearrange("p (a d) -> p a d", a=4), op=ALU.add), "v")


def emit_ffn_stats(P, zf, psum, pbf):
    block = P.block
    scr = P.t("scr4k")
    base_v = P.val("v")

    for tg in range(4):
        c0 = 1 + tg * 512
        for ch in range(8):
            cc = c0 if ch < 4 else c0 - 1
            sl = ch % 2

            @block.vector
            def _(v, tg=tg, ch=ch, cc=cc, sl=sl, base_v=base_v):
                if tg == 0 and ch == 0:
                    P.wait(v, "v", base_v)
                P.wait(v, "pe")
                P.inc(v.tensor_copy(P.t("stB")[:, 0, sl, :], zf[:, ch, cc:cc + 512]), "v")
            zcB = P.val("v")

            @block.scalar
            def _(s, tg=tg, ch=ch, cc=cc, sl=sl):
                P.wait(s, "pe")
                P.inc(s.activation(out=P.t("stB")[:, 1, sl, :], in_=zf[:, ch, cc:cc + 512],
                                   func=AF.Square), "s")
            sqB = P.val("s")

            @block.tensor
            def _(t, tg=tg, ch=ch, sl=sl, zcB=zcB, sqB=sqB):
                P.wait(t, "v", zcB)
                P.wait(t, "s", sqB)
                if ch == 0:
                    P.wait(t, "v")
                t.matmul(psum[0][0:1, :], P.t("ones_col")[:], P.t("stB")[:, 0, sl, :],
                         start=(ch == 0), stop=(ch == 7))
                inst = t.matmul(psum[1][0:1, :], P.t("ones_col")[:], P.t("stB")[:, 1, sl, :],
                                start=(ch == 0), stop=(ch == 7))
                P.inc(inst, "pe")
        mm_done = P.val("pe")

        @block.vector
        def _(v, tg=tg, mm_done=mm_done):
            P.wait(v, "pe", mm_done)
            v.tensor_scalar(out=scr[0:1, 0:512], in0=psum[0][0:1, :], scalar1=1.0 / C, scalar2=0.0,
                            op0=ALU.mult, op1=ALU.add)
            v.tensor_scalar(out=scr[0:1, 512:1024], in0=psum[1][0:1, :], scalar1=1.0 / C, scalar2=1e-5,
                            op0=ALU.mult, op1=ALU.add)
            v.tensor_copy(P.t("mu_bf")[0:1, tg * 512:(tg + 1) * 512], scr[0:1, 0:512])
            v.nop(cycle_cnt=32)
            v.tensor_tensor(out=scr[0:1, 0:512], in0=scr[0:1, 0:512], in1=scr[0:1, 0:512],
                            op=ALU.mult)
            v.nop(cycle_cnt=32)
            P.inc(v.tensor_tensor(out=scr[0:1, 512:1024], in0=scr[0:1, 512:1024],
                                  in1=scr[0:1, 0:512], op=ALU.subtract), "v")
        var_done = P.val("v")

        @block.scalar
        def _(s, tg=tg, var_done=var_done):
            P.wait(s, "v", var_done)
            s.activation(out=scr[0:1, 0:512], in_=scr[0:1, 512:1024], func=AF.Ln)
            s.activation(out=P.t("sig_bf")[0:1, tg * 512:(tg + 1) * 512], in_=scr[0:1, 0:512],
                         func=AF.Exp, scale=0.5)
            P.inc(s.activation(out=scr[0:1, 512:1024], in_=scr[0:1, 0:512],
                               func=AF.Exp, scale=-0.5), "s")
        inv_done = P.val("s")

        @block.tensor
        def _(t, tg=tg, inv_done=inv_done):
            P.wait(t, "s", inv_done)
            if tg > 0:
                P.wait(t, "v")
            for i in range(4):
                inst = t.transpose(psum[3][0:128, i:i + 1],
                                   scr[0:1, 512 + i * 128:512 + (i + 1) * 128],
                                   P.t("ident_f")[0:1, 0:1])
            P.inc(inst, "pe")
        ic_done = P.val("pe")

        @block.vector
        def _(v, tg=tg, ic_done=ic_done):
            P.wait(v, "pe", ic_done)
            P.inc(v.tensor_copy(P.t("invcol")[:, tg * 4:(tg + 1) * 4], psum[3][0:128, 0:4]), "v")


def emit_w1_pair(P, ly, pair, din, zf, psum, pbf, h4_hbm):
    block = P.block
    scr = P.t("scr4k")
    par = pair % 2
    sa, sg = 2 * par, 2 * par + 1
    a0, g0 = pair * SLW, HID + pair * SLW

    @block.gpsimd
    def _(g):
        P.wait(g, "pe")
        P.inc(g.dma_start(out=P.t("w1s")[:, sa, :, :],
                          in_=din["w1"][ly][:, a0:a0 + SLW].rearrange("(ch p) n -> p ch n", p=128)), "wl", 16)
        P.inc(g.dma_start(out=P.t("w1s")[:, sg, :, :],
                          in_=din["w1"][ly][:, g0:g0 + SLW].rearrange("(ch p) n -> p ch n", p=128)), "wl", 16)
    wsl_done = P.val("wl")

    @block.sync
    def _(sync):
        P.wait(sync, "v")
        P.inc(sync.dma_start(out=scr[0:1, 0:SLW], in_=din["b1"][ly][:, a0:a0 + SLW]), "b1", 16)
        P.inc(sync.dma_start(out=scr[0:1, 512:512 + SLW], in_=din["b1"][ly][:, g0:g0 + SLW]), "b1", 16)
    b1_done = P.val("b1")

    @block.tensor
    def _(t, wsl_done=wsl_done):
        P.wait(t, "wl", wsl_done)
        P.wait(t, "v")
        for ch in range(8):
            t.matmul(psum[0][0:1, 0:SLW], P.t("lbcol")[:, ch:ch + 1], P.t("w1s")[:, sa, ch, :],
                     start=(ch == 0), stop=(ch == 7))
            inst = t.matmul(psum[1][0:1, 0:SLW], P.t("lbcol")[:, ch:ch + 1], P.t("w1s")[:, sg, ch, :],
                            start=(ch == 0), stop=(ch == 7))
        P.inc(inst, "pe")
    cmm_done = P.val("pe")

    @block.vector
    def _(v, b1_done=b1_done, cmm_done=cmm_done):
        P.wait(v, "b1", b1_done)
        P.wait(v, "pe", cmm_done)
        v.tensor_tensor(out=P.t("crow")[0:1, 2 * par, :SLW], in0=psum[0][0:1, 0:SLW],
                        in1=scr[0:1, 0:SLW], op=ALU.add)
        P.inc(v.tensor_tensor(out=P.t("crow")[0:1, 2 * par + 1, :SLW], in0=psum[1][0:1, 0:SLW],
                              in1=scr[0:1, 512:512 + SLW], op=ALU.add), "v")
        for ch in range(8):
            v.tensor_scalar(out=P.t("w1s")[:, sa, ch, :], in0=P.t("w1s")[:, sa, ch, :],
                            scalar1=P.t("lgcol")[:, ch:ch + 1], scalar2=0.0, op0=ALU.mult, op1=ALU.add)
            inst = v.tensor_scalar(out=P.t("w1s")[:, sg, ch, :], in0=P.t("w1s")[:, sg, ch, :],
                                   scalar1=P.t("lgcol")[:, ch:ch + 1], scalar2=0.0, op0=ALU.mult, op1=ALU.add)
        P.inc(inst, "v")
    scale_done = P.val("v")

    @block.tensor
    def _(t, scale_done=scale_done):
        P.wait(t, "v", scale_done)
        for ch in range(8):
            t.matmul(psum[0][0:1, 0:SLW], P.t("ones_col")[:], P.t("w1s")[:, sa, ch, :],
                     start=(ch == 0), stop=(ch == 7))
            inst = t.matmul(psum[1][0:1, 0:SLW], P.t("ones_col")[:], P.t("w1s")[:, sg, ch, :],
                            start=(ch == 0), stop=(ch == 7))
        P.inc(inst, "pe")
    bmm_done = P.val("pe")

    @block.vector
    def _(v, bmm_done=bmm_done):
        P.wait(v, "pe", bmm_done)
        v.tensor_scalar(out=P.t("nbrow")[0:1, 2 * par, :SLW], in0=psum[0][0:1, 0:SLW],
                        scalar1=-1.0, scalar2=0.0, op0=ALU.mult, op1=ALU.add)
        P.inc(v.tensor_scalar(out=P.t("nbrow")[0:1, 2 * par + 1, :SLW], in0=psum[1][0:1, 0:SLW],
                              scalar1=-1.0, scalar2=0.0, op0=ALU.mult, op1=ALU.add), "v")
    rows_done = P.val("v")

    for lt in range(NLT):
        l0 = lt * 128

        @block.vector
        def _(v, lt=lt, l0=l0):
            P.wait(v, "s")
            v.tensor_copy(P.t("zcast")[:, 0:4, :], zf[:, 0:4, 1 + l0:1 + l0 + 128])
            P.inc(v.tensor_copy(P.t("zcast")[:, 4:8, :], zf[:, 4:8, l0:l0 + 128]), "v")
            if lt == 5 and getattr(P, "snap_mlp", None) == ly and pair == getattr(P, "snap_pair", 0):
                P.inc(v.tensor_copy(P.t("stB")[:].rearrange("p a b d -> p (a b d)")[:, 0:1024],
                                    P.t("zcast")[:].rearrange("p a d -> p (a d)")), "v")
        zc_done = P.val("v")

        @block.tensor
        def _(t, lt=lt, l0=l0, zc_done=zc_done, rows_done=rows_done):
            P.wait(t, "v", zc_done)
            if lt == 0:
                P.wait(t, "v", rows_done)
            P.wait(t, "s")
            for sl, bank in ((sa, 5), (sg, 6)):
                for ch in range(8):
                    t.matmul(psum[bank][:, 0:SLW], P.t("zcast")[:, ch, :], P.t("w1s")[:, sl, ch, :],
                             start=(ch == 0), stop=False)
                t.matmul(psum[bank][:, 0:SLW], P.t("mu_bf")[0:1, l0:l0 + 128],
                         P.t("nbrow")[0:1, sl, :SLW], start=False, stop=False)
                inst = t.matmul(psum[bank][:, 0:SLW], P.t("sig_bf")[0:1, l0:l0 + 128],
                                P.t("crow")[0:1, sl, :SLW], start=False, stop=True)
            P.inc(inst, "pe")
        amm_done = P.val("pe")

        @block.scalar
        def _(s, lt=lt, amm_done=amm_done):
            P.wait(s, "pe", amm_done)
            P.wait(s, "v")
            if lt == 5 and getattr(P, "snap_mlp", None) == ly and pair == getattr(P, "snap_pair", 0):
                s.activation(out=P.t("x1t")[:, 0, 0:SLW], in_=psum[5][:, 0:SLW], func=AF.Copy)
                s.activation(out=P.t("x1t")[:, 1, 0:SLW], in_=psum[6][:, 0:SLW], func=AF.Copy)
            P.inc(s.activation(out=P.t("ggscr")[:, 0:SLW], in_=psum[6][:, 0:SLW], func=AF.Gelu,
                               scale=P.t("invcol")[:, lt:lt + 1]), "s")
        gg_done = P.val("s")

        @block.vector
        def _(v, lt=lt, gg_done=gg_done):
            P.wait(v, "s", gg_done)
            P.inc(v.scalar_tensor_tensor(out=P.t("h4k")[:, lt, 0:SLW], in0=psum[5][:, 0:SLW],
                                         scalar=P.t("invcol")[:, lt:lt + 1],
                                         in1=P.t("ggscr")[:, 0:SLW], op0=ALU.mult, op1=ALU.mult), "v")
    h4_done = P.val("v")

    if getattr(P, "snap_mlp", None) == ly and pair == getattr(P, "snap_pair", 0):
        dh4 = P.nc.dram_tensor(f"dbg_h4dump{pair}", [128, NLT, SLW + 1], BF16, kind="ExternalOutput").ap()

        dps = P.nc.dram_tensor(f"dbg_psdump{pair}", [128, 2, 1024], F32, kind="ExternalOutput").ap()
        dw1 = P.nc.dram_tensor(f"dbg_w1dump{pair}", [128, 2, 8, SLW], BF16, kind="ExternalOutput").ap()

        @block.sync
        def _(sync, dh4=dh4, dps=dps, dw1=dw1):
            P.wait(sync, "v", h4_done)
            P.inc(sync.dma_start(out=dh4, in_=P.t("h4k")[:]), "st", 16)
            P.inc(sync.dma_start(out=dps, in_=P.t("x1t")[:]), "st", 16)
            P.inc(sync.dma_start(out=dw1, in_=P.t("w1s")[:, 2 * (pair % 2):2 * (pair % 2) + 2, :, :]), "st", 16)
            dzc = P.nc.dram_tensor(f"dbg_zcdump{pair}", [128, 1024], BF16, kind="ExternalOutput").ap()
            P.inc(sync.dma_start(out=dzc, in_=P.t("stB")[:].rearrange("p a b d -> p (a b d)")[:, 0:1024]), "st", 16)
        st_h4 = P.val("st")

        @block.vector
        def _(v, st_h4=st_h4):
            P.wait(v, "st", st_h4)
            P.inc(v.tensor_copy(P.t("x1t")[:, 1, 456:472], P.t("invcol")[:]), "v")
        P.snapdone = P.val("v")

    @block.sync
    def _(sync):
        P.wait(sync, "v", h4_done)
        P.inc(sync.dma_start(out=h4_hbm.rearrange("(lt p) n -> p lt n", p=128)[:, :, pair * (SLW + 1):(pair + 1) * (SLW + 1)],
                             in_=P.t("h4k")[:]), "h4s", 16)


def emit_w2_pair(P, ly, pair, din, zf, psum, pbf, h4_hbm):
    block = P.block
    par = pair % 2

    @block.gpsimd
    def _(g):
        P.wait(g, "pe")
        r0 = pair * SLW
        for sub in range(4):
            rs = r0 + sub * 128
            rn = min(128, r0 + SLW - rs)
            P.inc(g.dma_start(out=P.t("w2s")[0:rn, par, sub, :], in_=din["w2"][ly][rs:rs + rn, :]), "wl", 16)

    @block.sync
    def _(sync):
        P.wait(sync, "wl")
        P.wait(sync, "h4s", 16 * (NSL * ly + pair + 1))
        P.wait(sync, "pe")
        P.inc(sync.dma_start(out=P.t("h4k")[:],
                             in_=h4_hbm.rearrange("(lt p) n -> p lt n", p=128)[:, :, pair * (SLW + 1):(pair + 1) * (SLW + 1)]), "h4l", 16)
    h4l_done = P.val("h4l")

    for half in range(2):
        for ltt in range(8):
            lt = half * 8 + ltt

            @block.tensor
            def _(t, lt=lt, ltt=ltt, h4l_done=h4l_done):
                if ltt == 0:
                    P.wait(t, "h4l", h4l_done)
                P.wait(t, "v")
                for sub in range(4):
                    w = min(128, SLW - sub * 128)
                    inst = t.transpose(pbf(4)[0:w, sub * 128:(sub + 1) * 128],
                                       P.t("h4k")[:, lt, sub * 128:sub * 128 + w],
                                       P.t("ident_bf")[:])
                P.inc(inst, "pe")
            htp_done = P.val("pe")

            @block.vector
            def _(v, lt=lt, ltt=ltt, htp_done=htp_done):
                P.wait(v, "pe", htp_done)
                for sub in range(4):
                    w = min(128, SLW - sub * 128)
                    inst = v.tensor_copy(P.t("h4fmk")[0:w, sub, ltt * 128:(ltt + 1) * 128],
                                         pbf(4)[0:w, sub * 128:(sub + 1) * 128])
                P.inc(inst, "v")
        hfm_done = P.val("v")

        for tg in range(2):
            for ch in range(8):
                bank = 5 if (tg * 8 + ch) % 2 == 0 else 6

                @block.tensor
                def _(t, tg=tg, ch=ch, bank=bank, hfm_done=hfm_done):
                    P.wait(t, "v", hfm_done)
                    P.wait(t, "v")
                    P.wait(t, "s")
                    for sub in range(4):
                        w = min(128, SLW - sub * 128)
                        inst = t.matmul(psum[bank][:, 0:512],
                                        P.t("w2s")[0:w, par, sub, ch * 128:(ch + 1) * 128],
                                        P.t("h4fmk")[0:w, sub, tg * 512:(tg + 1) * 512],
                                        start=(sub == 0), stop=(sub == 3))
                    P.inc(inst, "pe")
                wmm = P.val("pe")
                emit_w2_add(P, half, tg, ch, zf, psum, bank, wmm,
                            last_pair=(pair == NSL - 1))


def emit_w2_add(P, half, tg, ch, zf, psum, bank, wmm, last_pair):
    block = P.block
    c0 = 1 + half * 1024 + tg * 512

    @block.vector
    def _(v, half=half, tg=tg, ch=ch, bank=bank, wmm=wmm):
        P.wait(v, "pe", wmm)
        if last_pair:
            P.inc(v.scalar_tensor_tensor(out=zf[:, ch, c0:c0 + 512], in0=psum[bank][:, 0:512],
                                         scalar=P.t("b2col")[:, ch:ch + 1], in1=zf[:, ch, c0:c0 + 512],
                                         op0=ALU.add, op1=ALU.add), "v")
        else:
            P.inc(v.tensor_tensor(out=zf[:, ch, c0:c0 + 512], in0=zf[:, ch, c0:c0 + 512],
                                  in1=psum[bank][:, 0:512], op=ALU.add), "v")


def emit_ln_tm(P, src, dst, wait_sem, out_bf16=True):
    st = P.t("stat")
    H = [(0, 8), (8, 16)]

    @P.block.vector
    def _(v):
        P.wait(v, *wait_sem)
        for lt in range(NLT):
            inst = v.tensor_reduce(out=st[:, lt:lt + 1], in_=src(lt), axis=AX.X, op=ALU.add, negate=True)
        P.inc(inst, "v")

    @P.block.scalar
    def _(s):
        P.wait(s, *wait_sem)
        for lt in range(NLT):
            P.inc(s.activation(out=P.t("scr4k")[:], in_=src(lt), func=AF.Square,
                               accum_out=st[:, 32 + lt:33 + lt]), "s")
    sq_done = P.val("s")

    @P.block.vector
    def _(v):
        P.wait(v, "s", sq_done)
        for a, b in H:
            v.tensor_scalar(out=st[:, a:b], in0=st[:, a:b], scalar1=1.0 / C, scalar2=0.0,
                            op0=ALU.mult, op1=ALU.add)
        v.nop(cycle_cnt=64)
        for a, b in H:
            v.tensor_tensor(out=st[:, 64 + a:64 + b], in0=st[:, a:b], in1=st[:, a:b], op=ALU.mult)
        for a, b in H:
            v.tensor_scalar(out=st[:, 48 + a:48 + b], in0=st[:, 32 + a:32 + b],
                            scalar1=1.0 / C, scalar2=1e-5, op0=ALU.mult, op1=ALU.add)
        for a, b in H:
            P.inc(v.tensor_tensor(out=st[:, 48 + a:48 + b], in0=st[:, 48 + a:48 + b],
                                  in1=st[:, 64 + a:64 + b], op=ALU.subtract), "v")
    var_done = P.val("v")

    @P.block.scalar
    def _(s):
        P.wait(s, "v", var_done)
        for a, b in H:
            s.activation(out=st[:, 64 + a:64 + b], in_=st[:, 48 + a:48 + b], func=AF.Ln)
        for a, b in H:
            P.inc(s.activation(out=st[:, 16 + a:16 + b], in_=st[:, 64 + a:64 + b],
                               func=AF.Exp, scale=-0.5), "s")
    sig_done = P.val("s")

    @P.block.vector
    def _(v):
        P.wait(v, "s", sig_done)
        if dst is not None:
            for lt in range(NLT):
                P.inc(v.scalar_tensor_tensor(out=dst(lt), in0=src(lt), scalar=st[:, lt:lt + 1],
                                             in1=bcol(P.t("stat"), 16 + lt, C), op0=ALU.add, op1=ALU.mult), "v")


def emit_dbg_tap(P, dbg, name, src_ap, shape, dtype, wait):
    d = dbg(name, shape, dtype)
    if d is None:
        return

    @P.block.sync
    def _(sync):
        P.wait(sync, *wait)
        P.inc(sync.dma_start(out=d, in_=src_ap), "st", 16)


def kernel(**inputs):
    hp = _host_prep(inputs)
    nc, dbg_names = build_program()
    x = hp.pop("x")
    in_maps = []
    for c in range(8):
        m = {"x": np.ascontiguousarray(x[c])}
        m.update(hp)
        in_maps.append(m)
    res = run_bass_kernel_spmd(nc, in_maps, core_ids=list(range(8)))
    out = np.stack([np.asarray(res.results[i]["y"]) for i in range(8)])
    if _DEBUG:
        kernel._dbg = [{k[4:]: np.asarray(v) for k, v in r.items() if k.startswith("dbg_")}
                       for r in res.results]
    return out
